# version 05
# baseline (speedup 1.0000x reference)
import math
import numpy as np
import ml_dtypes
import concourse.bass as bass
import concourse.mybir as mybir
from concourse.bass_utils import run_bass_kernel_spmd
from concourse.alu_op_type import AluOpType
from contextlib import ExitStack

F32 = mybir.dt.float32
BF16 = mybir.dt.bfloat16
U32 = mybir.dt.uint32
I32 = mybir.dt.int32
AF = mybir.ActivationFunctionType
AXX = mybir.AxisListType.X
AOP = AluOpType

B, T, D = 2, 4096, 4096
KVD, HD, NH, NKVH = 1024, 128, 32, 8
NT = T // 128
NS = 8
EPS = 1e-5
ISQ = 1.0 / math.sqrt(HD)
MASKNEG = -30000.0

DEBUG_TAPS = ()
PHASES = 6
NT_LIM = None
KVH_LIM = None
S_LIM = None
P4_STAGE = 9
SER_P1 = False
ENGS = ("sync", "tensor", "vector", "scalar", "gpsimd")


class S:
    def __init__(self, ctx, nc, name):
        self.h = ctx.enter_context(nc.semaphore(name))
        self.n = 0


class EQ:
    def __init__(self):
        self.q = {e: [] for e in ENGS}
        self.sig = {e: {} for e in ENGS}

    def wt(self, eng, sem, val):
        if val > 0:
            self.q[eng].append(lambda e, h=sem.h, v=val: e.wait_ge(h, v))

    def op(self, eng, fn, sem=None, k=1):
        if sem is None:
            self.q[eng].append(fn)
        else:
            sem.n += k
            self.sig[eng][sem] = sem.n
            self.q[eng].append(
                lambda e, f=fn, h=sem.h, kk=k: f(e).then_inc(h, kk))


def build():
    nc = bass.Bass("TRN2", target_bir_lowering=False,
                   detect_race_conditions=False)

    xb = nc.dram_tensor("xb", [T, D], F32, kind="ExternalInput").ap()
    wqT = nc.dram_tensor("wqT", [D, D], F32, kind="ExternalInput").ap()
    wkvT = nc.dram_tensor("wkvT", [D, 2 * KVD], F32, kind="ExternalInput").ap()
    woT = nc.dram_tensor("woT", [D, D], F32, kind="ExternalInput").ap()
    qw = nc.dram_tensor("qw", [D // 8, D], I32, kind="ExternalInput").ap()
    scal_c = nc.dram_tensor("scal_c", [128, 32], F32, kind="ExternalInput").ap()
    scal_r = nc.dram_tensor("scal_r", [1, D], F32, kind="ExternalInput").ap()
    zero_r = nc.dram_tensor("zero_r", [1, D], F32, kind="ExternalInput").ap()
    bias_r = nc.dram_tensor("bias_r", [1, D], F32, kind="ExternalInput").ap()
    cosT = nc.dram_tensor("cosT", [128, T], F32, kind="ExternalInput").ap()
    sinT = nc.dram_tensor("sinT", [128, T], F32, kind="ExternalInput").ap()
    idf = nc.dram_tensor("idf", [128, 128], F32, kind="ExternalInput").ap()
    idb = nc.dram_tensor("idb", [128, 128], BF16, kind="ExternalInput").ap()
    m01 = nc.dram_tensor("m01", [128, 512], F32, kind="ExternalInput").ap()
    madd = nc.dram_tensor("madd", [128, 512], F32, kind="ExternalInput").ap()

    out = nc.dram_tensor("out", [NS * 128, D], F32, kind="ExternalOutput").ap()

    def scratch(name, shape, dt):
        kind = "ExternalOutput" if name in DEBUG_TAPS else "Internal"
        return nc.dram_tensor(name, shape, dt, kind=kind).ap()

    xn_h = scratch("xn_h", [D, T], BF16)
    xn_l = scratch("xn_l", [D, T], BF16)
    xnp_h = scratch("xnp_h", [D, 1024], BF16)
    xnp_l = scratch("xnp_l", [D, 1024], BF16)
    q_h = scratch("q_h", [D, 1024], BF16)
    q_l = scratch("q_l", [D, 1024], BF16)
    k_h = scratch("k_h", [KVD, T], BF16)
    k_l = scratch("k_l", [KVD, T], BF16)
    v_h = scratch("v_h", [T, KVD], BF16)
    v_l = scratch("v_l", [T, KVD], BF16)
    ao_h = scratch("ao_h", [D, 1024], BF16)
    ao_l = scratch("ao_l", [D, 1024], BF16)
    g_s = scratch("g_s", [D, 1024], F32)
    sgt = scratch("sgt", [1024, D], F32)
    dbg1 = scratch("dbg1", [128, D], F32)
    dbg2 = scratch("dbg2", [128, D], BF16)
    dbg3 = scratch("dbg3", [128, D], BF16)
    dbg4 = scratch("dbg4", [128, D], F32)
    dbg5 = scratch("dbg5", [128, 8], F32)

    eq = EQ()
    ctx = ExitStack()
    with ctx:
        sem = lambda name: S(ctx, nc, name)
        SBg = lambda n, s, d: ctx.enter_context(nc.sbuf_tensor(n, s, d))

        idfs = SBg("idfs", [128, 128], F32)
        idbs = SBg("idbs", [128, 128], BF16)
        m01s = SBg("m01s", [128, 512], F32)
        madds = SBg("madds", [128, 512], F32)
        scalcs = SBg("scalcs", [128, 32], F32)
        l5 = SBg("l5", [5, D], BF16)
        r5 = SBg("r5", [5, 1024], BF16)
        scol = SBg("scol", [128, 8], F32)
        eps_ap = SBg("eps_ap", [128, 1], F32)
        isq_ap = SBg("isq_ap", [128, 1], F32)
        dmy1 = SBg("dmy1", [128, 1], F32)
        dmy2 = SBg("dmy2", [128, 1], F32)

        s_c0 = sem("s_c0"); s_ph = sem("s_ph"); s_x = sem("s_x")
        s_sq = sem("s_sq"); s_var = sem("s_var"); s_rs = sem("s_rs")
        s_bln = sem("s_bln"); s_apl = sem("s_apl"); s_sp = sem("s_sp")
        s_tr = sem("s_tr"); s_ev = sem("s_ev"); s_st = sem("s_st")
        s_l5 = sem("s_l5"); s_w = sem("s_w"); s_ws = sem("s_ws")
        s_mv = sem("s_mv"); s_mm = sem("s_mm"); s_tm = sem("s_tm")
        s_rope = sem("s_rope"); s_kv = sem("s_kv"); s_u = sem("s_u")
        s_g = sem("s_g"); s_qa = sem("s_qa"); s_sev = sem("s_sev")
        s_sm = sem("s_sm"); s_e1 = sem("s_e1"); s_e2 = sem("s_e2")
        s_esp = sem("s_esp"); s_tt = sem("s_tt"); s_tte = sem("s_tte")
        s_av = sem("s_av"); s_avk = sem("s_avk"); s_ao = sem("s_ao")
        s_sg = sem("s_sg"); s_o = sem("s_o"); s_vv = sem("s_vv")
        s_wsb = sem("s_wsb")

        phase_no = [0]

        def vbounce(sm):
            eq.wt("scalar", sm, sm.n)
            aop(lambda e: e.activation(out=dmy2[:], in_=dmy1[:],
                                       func=AF.Copy), s_wsb)
            eq.wt("vector", s_wsb, s_wsb.n)

        def barrier():
            snap = {e2: [(s, s.n) for s in eq.sig[e2]] for e2 in ENGS}
            for e1 in ENGS:
                for e2 in ENGS:
                    if e1 == e2:
                        continue
                    for s, v in snap[e2]:
                        eq.wt(e1, s, v)

        def dma(eng, o, i_, sm):
            eq.op(eng, lambda e, oo=o, ii=i_: e.dma_start(out=oo, in_=ii),
                  sm, 16)

        def split(src, hi, lo, sm=None):
            eq.op("vector", lambda e, a=src, b=hi: e.tensor_copy(b, a), s_vv)
            eq.wt("vector", s_vv, s_vv.n)
            eq.op("vector",
                  lambda e, a=src, b=hi, c=lo: e.scalar_tensor_tensor(
                      out=c, in0=a, scalar=1.0, in1=b, op0=AOP.mult,
                      op1=AOP.subtract), sm)

        def mm3(o, lh, ll, rh, rl, start, stop, sm=None, k=1):
            eq.op("tensor", lambda e, oo=o, a=lh, b=rh, st=start:
                  e.matmul(oo, a, b, start=st, stop=False))
            eq.op("tensor", lambda e, oo=o, a=lh, b=rl:
                  e.matmul(oo, a, b, start=False, stop=False))
            eq.op("tensor", lambda e, oo=o, a=ll, b=rh, sp=stop:
                  e.matmul(oo, a, b, start=False, stop=sp), sm, k)

        def vop(fn, sm=None, k=1):
            eq.op("vector", fn, sm, k)

        def aop(fn, sm=None, k=1):
            eq.op("scalar", fn, sm, k)

        def top(fn, sm=None, k=1):
            eq.op("tensor", fn, sm, k)

        p0 = ExitStack()
        SB0 = lambda n, s, d: p0.enter_context(nc.sbuf_tensor(n, s, d))
        rowf = SB0("rowf", [1, D], F32)
        rowf2 = SB0("rowf2", [1, D], F32)
        rowf3 = SB0("rowf3", [1, D], F32)
        rowbh = SB0("rowbh", [1, D], BF16)
        rowbl = SB0("rowbl", [1, D], BF16)
        rowch = SB0("rowch", [1, D], BF16)
        rowcl = SB0("rowcl", [1, D], BF16)
        row1 = SB0("row1", [1, 1024], BF16)
        for hb, sb_ in ((idf, idfs), (idb, idbs), (m01, m01s), (madd, madds),
                        (scal_c, scalcs)):
            dma("sync", sb_[:], hb[:], s_c0)
        dma("sync", rowf[:], scal_r[:], s_c0)
        dma("sync", rowf2[:], zero_r[:], s_c0)
        dma("sync", rowf3[:], bias_r[:], s_c0)
        eq.wt("vector", s_c0, s_c0.n)
        vop(lambda e: e.reciprocal(rowf[:], rowf[:]), s_vv)
        eq.wt("vector", s_vv, s_vv.n)
        vop(lambda e: e.scalar_tensor_tensor(
            out=rowf2[:], in0=rowf2[:], scalar=-1.0, in1=rowf[:],
            op0=AOP.mult, op1=AOP.mult), s_vv)
        eq.wt("vector", s_vv, s_vv.n)
        vop(lambda e: e.tensor_tensor(out=rowf3[:], in0=rowf3[:], in1=rowf[:],
                                      op=AOP.mult), s_vv)
        eq.wt("vector", s_vv, s_vv.n)
        vop(lambda e: e.tensor_copy(rowbh[:], rowf2[:]), s_vv)
        eq.wt("vector", s_vv, s_vv.n)
        vop(lambda e: e.scalar_tensor_tensor(
            out=rowbl[:], in0=rowf2[:], scalar=1.0, in1=rowbh[:],
            op0=AOP.mult, op1=AOP.subtract), s_vv)
        eq.wt("vector", s_vv, s_vv.n)
        vop(lambda e: e.tensor_copy(rowch[:], rowf3[:]), s_vv)
        eq.wt("vector", s_vv, s_vv.n)
        vop(lambda e: e.scalar_tensor_tensor(
            out=rowcl[:], in0=rowf3[:], scalar=1.0, in1=rowch[:],
            op0=AOP.mult, op1=AOP.subtract), s_vv)
        eq.wt("vector", s_vv, s_vv.n)
        vop(lambda e: e.memset(row1[:], 1.0))
        vop(lambda e: e.memset(eps_ap[:], EPS))
        vop(lambda e: e.memset(isq_ap[:], ISQ), s_l5)
        eq.wt("sync", s_l5, s_l5.n)
        dma("sync", l5[0:1, :], rowbh[:], s_l5)
        dma("sync", l5[1:2, :], rowbh[:], s_l5)
        dma("sync", l5[2:3, :], rowbl[:], s_l5)
        dma("sync", l5[3:4, :], rowch[:], s_l5)
        dma("sync", l5[4:5, :], rowcl[:], s_l5)
        dma("sync", r5[3:4, :], row1[:], s_l5)
        dma("sync", r5[4:5, :], row1[:], s_l5)
        eq.wt("sync", s_c0, s_c0.n)
        eq.wt("sync", s_l5, s_l5.n)
        p0.close()
        barrier()

        with ExitStack() as p1:
            SB1 = lambda n, s, d: p1.enter_context(nc.sbuf_tensor(n, s, d))
            PS1 = lambda n, s, d: p1.enter_context(nc.psum_tensor(n, s, d))
            xtile = [SB1(f"xt{i}", [128, D], F32) for i in range(2)]
            xn32 = [SB1(f"xn32{i}", [128, D], F32) for i in range(2)]
            xnh = [SB1(f"xnh{i}", [128, D], BF16) for i in range(2)]
            xnl = [SB1(f"xnl{i}", [128, D], BF16) for i in range(2)]
            stg = [SB1(f"stg{i}", [128, 1024], BF16) for i in range(4)]
            st1 = SB1("st1", [128, 1], F32)
            st2 = SB1("st2", [128, 1], F32)
            st3 = SB1("st3", [128, 1], F32)
            st4 = SB1("st4", [128, 1], F32)
            st5 = SB1("st5", [128, 1], F32)
            st6 = SB1("st6", [128, 1], F32)
            srow_f = SB1("srow_f", [1, 1024], F32)
            srowbh = SB1("srowbh", [1, 1024], BF16)
            srowbl = SB1("srowbl", [1, 1024], BF16)
            trb = [PS1(f"trb{i}", [128, 1024], BF16) for i in range(4)]
            spsum = PS1("spsum", [1, 1024], F32)

            ngrp = [0]
            tr_after_tile = {}

            def do_groups(src_hi, src_lo, k, perm):
                s_own = k // 4
                for src, dsth in ((src_hi, xnp_h if perm else xn_h),
                                  (src_lo, xnp_l if perm else xn_l)):
                    for g in range(4):
                        n = ngrp[0]
                        bank = trb[n % 4]
                        eq.wt("tensor", s_st, 16 * max(0, n - 3))
                        for u in range(8):
                            cj = 8 * g + u
                            if perm:
                                c2, j2 = cj // 8, cj % 8
                                ap_in = src[:, 1024 * c2 + j2:
                                            1024 * c2 + j2 + 1017:8]
                            else:
                                ap_in = src[:, 128 * cj:128 * cj + 128]
                            last = (u == 7)
                            top(lambda e, o=bank[:, 128 * u:128 * (u + 1)],
                                a=ap_in: e.transpose(o, a, idbs[:]),
                                s_tr if last else None)
                        eq.wt("scalar", s_tr, s_tr.n)
                        eq.wt("scalar", s_st, 16 * max(0, n - 3))
                        stgb = stg[n % 4]
                        aop(lambda e, o=stgb[:], a=bank[:]:
                            e.activation(out=o, in_=a, func=AF.Copy), s_ev)
                        eq.wt("gpsimd", s_ev, s_ev.n)
                        if perm:
                            dst_ap = dsth[1024 * g:1024 * (g + 1),
                                          128 * s_own:128 * s_own + 128]
                        else:
                            dst_ap = dsth[1024 * g:1024 * (g + 1),
                                          128 * k:128 * k + 128]
                        d3 = dst_ap.rearrange("(s p) f -> p s f", s=8, p=128)
                        i3 = stgb[:].rearrange("p (s f) -> p s f", s=8, f=128)
                        dma("gpsimd", d3, i3, s_st)
                        ngrp[0] += 1

            for k in range(NT):
                p = k % 2
                eq.wt("sync", s_apl, max(0, k - 1))
                dma("sync", xtile[p][:], xb[128 * k:128 * (k + 1), :], s_x)
                eq.wt("vector", s_x, 16 * (k + 1))
                vop(lambda e, a=xtile[p][:]: e.reduce_sum(
                    st1[:], a, axis=AXX, negate=True), s_vv)
                eq.wt("vector", s_vv, s_vv.n)
                eq.wt("scalar", s_x, 16 * (k + 1))
                eq.wt("scalar", s_sp, max(0, k - 1))
                eq.wt("scalar", s_var, k)
                aop(lambda e, o=xn32[p][:], a=xtile[p][:]: e.activation(
                    out=o, in_=a, func=AF.Square, accum_out=st2[:]), s_sq)
                eq.wt("vector", s_sq, s_sq.n)
                vop(lambda e: e.tensor_scalar(out=st3[:], in0=st1[:],
                                              scalar1=1.0 / D, scalar2=None,
                                              op0=AOP.mult), s_vv)
                eq.wt("vector", s_vv, s_vv.n)
                vop(lambda e: e.tensor_tensor(out=st4[:], in0=st3[:],
                                              in1=st3[:], op=AOP.mult), s_vv)
                eq.wt("vector", s_vv, s_vv.n)
                vop(lambda e: e.scalar_tensor_tensor(
                    out=st4[:], in0=st2[:], scalar=1.0 / D, in1=st4[:],
                    op0=AOP.mult, op1=AOP.subtract), s_var)
                eq.wt("scalar", s_var, s_var.n)
                if k == 0:
                    eq.wt("scalar", s_l5, s_l5.n)
                aop(lambda e: e.activation(out=st5[:], in_=st4[:],
                                           func=AF.Sqrt, bias=eps_ap[:],
                                           scale=1.0), s_rs)
                eq.wt("vector", s_rs, s_rs.n)
                vop(lambda e: e.reciprocal(st5[:], st5[:]), s_vv)
                eq.wt("vector", s_vv, s_vv.n)
                vop(lambda e: e.tensor_tensor(out=st6[:], in0=st3[:],
                                              in1=st5[:], op=AOP.mult), s_bln)
                eq.wt("vector", s_bln, s_bln.n)
                eq.wt("vector", s_sq, s_sq.n)
                vop(lambda e, o=xn32[p][:], a=xtile[p][:]: e.tensor_scalar(
                    out=o, in0=a, scalar1=st5[:], scalar2=st6[:],
                    op0=AOP.mult, op1=AOP.add), s_apl)
                eq.wt("vector", s_apl, s_apl.n)
                if k % 4 == 0:
                    vop(lambda e, a=xn32[p][:], c=scol[:, k // 4:k // 4 + 1]:
                        e.reduce_sum(c, a, axis=AXX))
                eq.wt("vector", s_tr, tr_after_tile.get(k - 2, 0))
                split(xn32[p][:], xnh[p][:], xnl[p][:], s_sp)
                if k == 0 and "dbg1" in DEBUG_TAPS:
                    eq.wt("gpsimd", s_sp, s_sp.n)
                    dma("gpsimd", dbg1[:], xn32[p][:], s_st)
                    dma("gpsimd", dbg2[:], xnh[p][:], s_st)
                    dma("gpsimd", dbg3[:], xnl[p][:], s_st)
                    dma("gpsimd", dbg4[:], xtile[p][:], s_st)
                    for di_, stx in enumerate((st1, st2, st3, st4, st5, st6)):
                        vop(lambda e, oo=scol[:, di_:di_ + 1], ii=stx[:]:
                            e.tensor_copy(oo, ii), s_sp)
                    eq.wt("gpsimd", s_sp, s_sp.n)
                    dma("gpsimd", dbg5[:], scol[:], s_st)
                eq.wt("tensor", s_sp, s_sp.n)
                do_groups(xnh[p], xnl[p], k, perm=False)
                if k % 4 == 0:
                    do_groups(xnh[p], xnl[p], k, perm=True)
                tr_after_tile[k] = s_tr.n
                if SER_P1:
                    eq.wt("gpsimd", s_st, s_st.n)
                    barrier()
                if NT_LIM is not None and k + 1 >= NT_LIM:
                    break
            eq.wt("tensor", s_sp, s_sp.n)
            for s in range(8):
                top(lambda e, o=spsum[0:1, 128 * s:128 * (s + 1)],
                    a=scol[:, s:s + 1]: e.transpose(o, a, idfs[:]),
                    s_tr if s == 7 else None)
            eq.wt("vector", s_tr, s_tr.n)
            vop(lambda e: e.tensor_copy(srow_f[:], spsum[0:1, :]), s_vv)
            eq.wt("vector", s_vv, s_vv.n)
            vop(lambda e: e.tensor_copy(srowbh[:], srow_f[:]), s_vv)
            eq.wt("vector", s_vv, s_vv.n)
            vop(lambda e: e.scalar_tensor_tensor(
                out=srowbl[:], in0=srow_f[:], scalar=1.0,
                in1=srowbh[:], op0=AOP.mult, op1=AOP.subtract), s_l5)
            eq.wt("sync", s_l5, s_l5.n)
            dma("sync", r5[0:1, 0:1024], srowbh[:], s_l5)
            dma("sync", r5[1:2, 0:1024], srowbl[:], s_l5)
            dma("sync", r5[2:3, 0:1024], srowbh[:], s_l5)
            eq.wt("sync", s_l5, s_l5.n)
            eq.wt("gpsimd", s_st, s_st.n)
            barrier()

        if PHASES >= 2:
            with ExitStack() as p2:
                SB2 = lambda n, s, d: p2.enter_context(nc.sbuf_tensor(n, s, d))
                PS2 = lambda n, s, d: p2.enter_context(nc.psum_tensor(n, s, d))
                coss = SB2("coss", [128, T], F32)
                sins = SB2("sins", [128, T], F32)
                xnb_h = SB2("xnb_h", [128, 32 * 512], BF16)
                xnb_l = SB2("xnb_l", [128, 32 * 512], BF16)
                wf = [SB2(f"wf{i}", [128, 1024], F32) for i in range(2)]
                whb = [SB2(f"whb{i}", [128, 1024], BF16) for i in range(2)]
                wlb = [SB2(f"wlb{i}", [128, 1024], BF16) for i in range(2)]
                rt1 = SB2("rt1", [128, 512], F32)
                rt2 = SB2("rt2", [128, 512], F32)
                rt3 = SB2("rt3", [128, 512], F32)
                sth = SB2("sth", [128, 512], BF16)
                stl = SB2("stl", [128, 512], BF16)
                pk = [PS2(f"pk{i}", [128, 512], F32) for i in range(8)]
                ptr = pk[0][:].bitcast(BF16)

                dma("sync", coss[:], cosT[:], s_c0)
                dma("sync", sins[:], sinT[:], s_c0)
                eq.wt("vector", s_c0, s_c0.n)

                nW = [0]
                u_sig = []

                def wload(src_ap):
                    pb = nW[0] % 2
                    eq.wt("gpsimd", s_ws, max(0, nW[0] - 1))
                    dma("gpsimd", wf[pb][:], src_ap, s_w)
                    eq.wt("vector", s_w, s_w.n)
                    if nW[0] >= 1 and u_sig:
                        eq.wt("vector", *u_sig[min(nW[0], len(u_sig)) - 1])
                    split(wf[pb][:], whb[pb][:], wlb[pb][:], s_ws)
                    eq.wt("scalar", s_ws, s_ws.n)
                    aop(lambda e: e.activation(out=dmy2[:], in_=dmy1[:],
                                               func=AF.Copy), s_wsb)
                    nW[0] += 1
                    return whb[pb], wlb[pb]

                def rope_ops(psrc, cos_ap, sin_ap, threed):
                    if threed:
                        o1 = rt1[:, 0:512].rearrange("p (u f) -> p u f", u=4, f=128)
                        i0 = psrc.rearrange("p (u f) -> p u f", u=4, f=128)
                        o3 = rt3[:, 0:512].rearrange("p (u f) -> p u f", u=4, f=128)
                        i2 = rt2[:, 0:512].rearrange("p (u f) -> p u f", u=4, f=128)
                    else:
                        o1, i0, o3, i2 = rt1[:, 0:512], psrc, rt3[:, 0:512], \
                            rt2[:, 0:512]
                    vop(lambda e: e.tensor_tensor(out=o1, in0=i0, in1=cos_ap,
                                                  op=AOP.mult))
                    vop(lambda e, a=psrc: e.tensor_scalar(
                        out=rt2[0:64, 0:512], in0=a[64:128, :], scalar1=-1.0,
                        scalar2=None, op0=AOP.mult))
                    vop(lambda e, a=psrc: e.tensor_copy(rt2[64:128, 0:512],
                                                        a[0:64, :]), s_vv)
                    eq.wt("vector", s_vv, s_vv.n)
                    vop(lambda e: e.tensor_tensor(out=o3, in0=i2, in1=sin_ap,
                                                  op=AOP.mult), s_vv)
                    eq.wt("vector", s_vv, s_vv.n)
                    vop(lambda e: e.tensor_tensor(out=rt1[:, 0:512],
                                                  in0=rt1[:, 0:512],
                                                  in1=rt3[:, 0:512], op=AOP.add),
                        s_vv)
                    eq.wt("vector", s_vv, s_vv.n)

                def stage_store(src32, dh_ap, dl_ap):
                    eq.wt("vector", s_kv, s_kv.n)
                    split(src32, sth[:, 0:512], stl[:, 0:512], s_rope)
                    eq.wt("gpsimd", s_rope, s_rope.n)
                    dma("gpsimd", dh_ap, sth[:, 0:512], s_kv)
                    dma("gpsimd", dl_ap, stl[:, 0:512], s_kv)

                for tb in range(8):
                    if u_sig:
                        eq.wt("sync", *u_sig[-1])
                    for ic in range(32):
                        dma("sync", xnb_h[:, 512 * ic:512 * (ic + 1)],
                            xn_h[128 * ic:128 * (ic + 1), 512 * tb:512 * (tb + 1)],
                            s_mv)
                        dma("sync", xnb_l[:, 512 * ic:512 * (ic + 1)],
                            xn_l[128 * ic:128 * (ic + 1), 512 * tb:512 * (tb + 1)],
                            s_mv)
                    mv_target = s_mv.n
                    for ob in range(2):
                        for ic in range(32):
                            wh, wl = wload(wkvT[128 * ic:128 * (ic + 1),
                                                1024 * ob:1024 * (ob + 1)])
                            eq.wt("tensor", s_wsb, s_wsb.n)
                            if ic == 0:
                                eq.wt("tensor", s_mv, mv_target)
                                eq.wt("tensor", s_rope, s_rope.n)
                                eq.wt("tensor", s_ev, s_ev.n)
                                eq.wt("tensor", s_kv, s_kv.n)
                            xh = xnb_h[:, 512 * ic:512 * (ic + 1)]
                            xl = xnb_l[:, 512 * ic:512 * (ic + 1)]
                            for ot in range(8):
                                fin = s_mm if ic == 31 else s_tm
                                mm3(pk[ot][:], wh[:, 128 * ot:128 * (ot + 1)],
                                    wl[:, 128 * ot:128 * (ot + 1)], xh, xl,
                                    start=(ic == 0), stop=(ic == 31),
                                    sm=fin if ot == 7 else None)
                            u_sig.append((s_mm, s_mm.n) if ic == 31
                                         else (s_tm, s_tm.n))
                        eq.wt("vector", s_mm, s_mm.n)
                        vbounce(s_mm)
                        if ob == 0:
                            for ot in range(8):
                                rope_ops(pk[ot][:],
                                         coss[:, 512 * tb:512 * (tb + 1)],
                                         sins[:, 512 * tb:512 * (tb + 1)], False)
                                stage_store(
                                    rt1[:, 0:512],
                                    k_h[128 * ot:128 * (ot + 1),
                                        512 * tb:512 * (tb + 1)],
                                    k_l[128 * ot:128 * (ot + 1),
                                        512 * tb:512 * (tb + 1)])
                        else:
                            for ot in range(8):
                                eq.wt("vector", s_kv, s_kv.n)
                                split(pk[ot][:], sth[:, 0:512], stl[:, 0:512],
                                      s_rope)
                                eq.wt("tensor", s_rope, s_rope.n)
                                eq.wt("tensor", s_ev, s_ev.n)
                                for half, stsrc in ((0, sth), (1, stl)):
                                    for u in range(4):
                                        top(lambda e,
                                            o=ptr[:, 128 * (4 * half + u):
                                                  128 * (4 * half + u) + 128],
                                            a=stsrc[:, 128 * u:128 * u + 128]:
                                            e.transpose(o, a, idbs[:]),
                                            s_tr if (half, u) == (1, 3) else None)
                                eq.wt("scalar", s_tr, s_tr.n)
                                eq.wt("scalar", s_kv, s_kv.n)
                                aop(lambda e: e.activation(
                                    out=sth[:, 0:512], in_=ptr[:, 0:512],
                                    func=AF.Copy), s_ev)
                                aop(lambda e: e.activation(
                                    out=stl[:, 0:512], in_=ptr[:, 512:1024],
                                    func=AF.Copy), s_ev)
                                eq.wt("gpsimd", s_ev, s_ev.n)
                                d3h = v_h[512 * tb:512 * (tb + 1),
                                          128 * ot:128 * (ot + 1)].rearrange(
                                              "(s p) f -> p s f", s=4, p=128)
                                d3l = v_l[512 * tb:512 * (tb + 1),
                                          128 * ot:128 * (ot + 1)].rearrange(
                                              "(s p) f -> p s f", s=4, p=128)
                                dma("gpsimd", d3h, sth[:, 0:512].rearrange(
                                    "p (s f) -> p s f", s=4, f=128), s_kv)
                                dma("gpsimd", d3l, stl[:, 0:512].rearrange(
                                    "p (s f) -> p s f", s=4, f=128), s_kv)


                for ob in range(4):
                    for th in range(2):
                        eq.wt("sync", *u_sig[-1])
                        eq.wt("scalar", *u_sig[-1])
                        for ic in range(32):
                            src_h = xn_h[128 * ic:128 * (ic + 1),
                                         2048 * th:2048 * (th + 1)].rearrange(
                                             "p (u w) -> p u w", u=4, w=512)[
                                                 :, :, 0:128]
                            src_l = xn_l[128 * ic:128 * (ic + 1),
                                         2048 * th:2048 * (th + 1)].rearrange(
                                             "p (u w) -> p u w", u=4, w=512)[
                                                 :, :, 0:128]
                            dma("sync",
                                xnb_h[:, 512 * ic:512 * (ic + 1)].rearrange(
                                    "p (u f) -> p u f", u=4, f=128), src_h, s_mv)
                            dma("sync",
                                xnb_l[:, 512 * ic:512 * (ic + 1)].rearrange(
                                    "p (u f) -> p u f", u=4, f=128), src_l, s_mv)
                        mv_target = s_mv.n
                        for ic in range(32):
                            wh, wl = wload(wqT[128 * ic:128 * (ic + 1),
                                               1024 * ob:1024 * (ob + 1)])
                            eq.wt("tensor", s_wsb, s_wsb.n)
                            if ic == 0:
                                eq.wt("tensor", s_mv, mv_target)
                                eq.wt("tensor", s_rope, s_rope.n)
                                eq.wt("tensor", s_ev, s_ev.n)
                                eq.wt("tensor", s_kv, s_kv.n)
                            xh = xnb_h[:, 512 * ic:512 * (ic + 1)]
                            xl = xnb_l[:, 512 * ic:512 * (ic + 1)]
                            for ot in range(8):
                                fin = s_mm if ic == 31 else s_tm
                                mm3(pk[ot][:], wh[:, 128 * ot:128 * (ot + 1)],
                                    wl[:, 128 * ot:128 * (ot + 1)], xh, xl,
                                    start=(ic == 0), stop=(ic == 31),
                                    sm=fin if ot == 7 else None)
                            u_sig.append((s_mm, s_mm.n) if ic == 31
                                         else (s_tm, s_tm.n))
                        eq.wt("vector", s_mm, s_mm.n)
                        vbounce(s_mm)
                        cos3 = coss.rearrange("p (m w) -> p m w", m=8, w=512)[
                            :, 4 * th:4 * th + 4, 0:128]
                        sin3 = sins.rearrange("p (m w) -> p m w", m=8, w=512)[
                            :, 4 * th:4 * th + 4, 0:128]
                        for ot in range(8):
                            head = 8 * ob + ot
                            rope_ops(pk[ot][:], cos3, sin3, True)
                            stage_store(rt1[:, 0:512],
                                        q_h[128 * head:128 * (head + 1),
                                            512 * th:512 * (th + 1)],
                                        q_l[128 * head:128 * (head + 1),
                                            512 * th:512 * (th + 1)])
                barrier()

        if PHASES >= 3:
            with ExitStack() as p3:
                SB3 = lambda n, s, d: p3.enter_context(nc.sbuf_tensor(n, s, d))
                PS3 = lambda n, s, d: p3.enter_context(nc.psum_tensor(n, s, d))
                qwc = [SB3(f"qwc{i}", [128, D], U32) for i in range(4)]
                utmp = SB3("utmp", [128, 512], U32)
                ub = [SB3(f"ub{i}", [128, 512], BF16) for i in range(2)]
                xph = [SB3(f"xph{i}", [128, 1024], BF16) for i in range(2)]
                xpl = [SB3(f"xpl{i}", [128, 1024], BF16) for i in range(2)]
                gst = [SB3(f"gst{i}", [128, 1024], F32) for i in range(2)]
                pf = [PS3(f"pf{i}", [128, 1024], F32) for i in range(4)]

                for c2 in range(4):
                    dma("sync", qwc[c2][:],
                        qw[128 * c2:128 * (c2 + 1), :].bitcast(U32), s_c0)
                c0t = s_c0.n
                eq.wt("vector", s_c0, c0t)
                eq.wt("vector", s_l5, s_l5.n)
                nu = [0]
                nsl = [0]
                e1_base = s_e1.n
                for ob in range(8):
                    for cj in range(32):
                        c2, j2 = cj // 8, cj % 8
                        pb = nu[0] % 2
                        eq.wt("sync", s_tm, max(0, s_tm.n - 1))
                        dma("sync", xph[pb][:],
                            xnp_h[128 * cj:128 * (cj + 1), :], s_mv)
                        dma("sync", xpl[pb][:],
                            xnp_l[128 * cj:128 * (cj + 1), :], s_mv)
                        mvt = s_mv.n
                        eq.wt("vector", s_tm, max(0, s_tm.n - 1))
                        vop(lambda e, a=qwc[c2][:, 512 * ob:512 * (ob + 1)],
                            sh=4 * j2: e.tensor_scalar(
                                out=utmp[:], in0=a, scalar1=sh, scalar2=15,
                                op0=AOP.logical_shift_right, op1=AOP.bitwise_and),
                            s_vv)
                        eq.wt("vector", s_vv, s_vv.n)
                        vop(lambda e, o=ub[pb][:]: e.tensor_copy(o, utmp[:]), s_u)
                        eq.wt("tensor", s_u, s_u.n)
                        eq.wt("tensor", s_mv, mvt)
                        if cj == 0:
                            eq.wt("tensor", s_e1, e1_base + 4 * ob)
                        for ot in range(4):
                            for th2 in range(2):
                                po = pf[ot][:, 512 * th2:512 * (th2 + 1)]
                                eq.op("tensor", lambda e, oo=po,
                                      a=ub[pb][:, 128 * ot:128 * (ot + 1)],
                                      b=xph[pb][:, 512 * th2:512 * (th2 + 1)],
                                      st=(cj == 0): e.matmul(oo, a, b, start=st,
                                                             stop=False))
                                last = (ot == 3 and th2 == 1)
                                eq.op("tensor", lambda e, oo=po,
                                      a=ub[pb][:, 128 * ot:128 * (ot + 1)],
                                      b=xpl[pb][:, 512 * th2:512 * (th2 + 1)]:
                                      e.matmul(oo, a, b, start=False, stop=False),
                                      s_tm if last else None)
                        nu[0] += 1
                    for ot in range(4):
                        ocol = 512 * ob + 128 * ot
                        for th2 in range(2):
                            po = pf[ot][:, 512 * th2:512 * (th2 + 1)]
                            last = (ot == 3 and th2 == 1)
                            eq.op("tensor", lambda e, oo=po,
                                  a=l5[:, ocol:ocol + 128],
                                  b=r5[:, 512 * th2:512 * (th2 + 1)]:
                                  e.matmul(oo, a, b, start=False, stop=True),
                                  s_mm if last else None)
                    eq.wt("scalar", s_mm, s_mm.n)
                    for ot in range(4):
                        otile = 4 * ob + ot
                        gpb = nsl[0] % 2
                        eq.wt("scalar", s_g, 16 * max(0, nsl[0] - 1))
                        aop(lambda e, o=gst[gpb][:], a=pf[ot][:],
                            sc=scalcs[:, otile:otile + 1]: e.activation(
                                out=o, in_=a, func=AF.Silu, scale=sc, bias=0.0),
                            s_e1)
                        eq.wt("gpsimd", s_e1, s_e1.n)
                        dma("gpsimd", g_s[128 * otile:128 * (otile + 1), :],
                            gst[gpb][:], s_g)
                        nsl[0] += 1
                eq.wt("gpsimd", s_g, s_g.n)
                barrier()

        if PHASES >= 4:
            with ExitStack() as p4:
                SB4 = lambda n, s, d: p4.enter_context(nc.sbuf_tensor(n, s, d))
                PS4 = lambda n, s, d: p4.enter_context(nc.psum_tensor(n, s, d))
                ksb_h = SB4("ksb_h", [128, T], BF16)
                ksb_l = SB4("ksb_l", [128, T], BF16)
                vsb_h = SB4("vsb_h", [128, T], BF16)
                vsb_l = SB4("vsb_l", [128, T], BF16)
                qsb_h = [SB4(f"qsbh{i}", [128, 128], BF16) for i in range(4)]
                qsb_l = [SB4(f"qsbl{i}", [128, 128], BF16) for i in range(4)]
                srow = SB4("srow", [128, T], F32)
                ebuf = SB4("ebuf", [128, T], F32)
                eh = [SB4(f"eh{i}", [128, T], BF16) for i in range(4)]
                el = [SB4(f"el{i}", [128, T], BF16) for i in range(4)]
                etsb_h = [SB4(f"etsbh{i}", [128, 512], BF16) for i in range(4)]
                etsb_l = [SB4(f"etsbl{i}", [128, 512], BF16) for i in range(4)]
                aostg_h = SB4("aostg_h", [128, 512], BF16)
                aostg_l = SB4("aostg_l", [128, 512], BF16)
                nm = SB4("nm", [128, 1], F32)
                nms = SB4("nms", [128, 1], F32)
                ell4 = SB4("ell4", [128, 1], F32)
                lnl = SB4("lnl", [128, 1], F32)
                b2 = SB4("b2", [128, 1], F32)
                psc = [PS4(f"psc{i}", [128, 512], F32) for i in range(2)]
                pet = [PS4(f"pet{i}", [128, 1024], BF16) for i in range(4)]
                pav = PS4("pav", [128, 512], F32)

                nkb = [0]
                for kvh in range(KVH_LIM or NKVH):
                    eq.wt("sync", s_av, s_av.n)
                    dma("sync", ksb_h[:], k_h[128 * kvh:128 * (kvh + 1), :], s_qa)
                    dma("sync", ksb_l[:], k_l[128 * kvh:128 * (kvh + 1), :], s_qa)
                    for kt in range(32):
                        dma("sync", vsb_h[:, 128 * kt:128 * (kt + 1)],
                            v_h[128 * kt:128 * (kt + 1),
                                128 * kvh:128 * (kvh + 1)], s_qa)
                        dma("sync", vsb_l[:, 128 * kt:128 * (kt + 1)],
                            v_l[128 * kt:128 * (kt + 1),
                                128 * kvh:128 * (kvh + 1)], s_qa)
                    kv_t = s_qa.n
                    for s in range(S_LIM or NS):
                        ext = (s + 1) * 512
                        sev_base = s_sev.n
                        if P4_STAGE < 1:
                            continue
                        for hh in range(4):
                            head = 4 * kvh + hh
                            eq.wt("sync", s_av, s_av.n)
                            dma("sync", qsb_h[hh][:],
                                q_h[128 * head:128 * (head + 1),
                                    128 * s:128 * (s + 1)], s_qa)
                            dma("sync", qsb_l[hh][:],
                                q_l[128 * head:128 * (head + 1),
                                    128 * s:128 * (s + 1)], s_qa)
                            q_t = s_qa.n
                            e2_before = s_e2.n
                            if P4_STAGE < 1.2:
                                continue
                            for kb in range(s + 1):
                                pb = nkb[0] % 2
                                nkb[0] += 1
                                eq.wt("tensor", s_qa, q_t)
                                eq.wt("tensor", s_sev, max(0, s_sev.n - 1))
                                mm3(psc[pb][:], qsb_h[hh][:], qsb_l[hh][:],
                                    ksb_h[:, 512 * kb:512 * (kb + 1)],
                                    ksb_l[:, 512 * kb:512 * (kb + 1)],
                                    start=True, stop=True, sm=s_mm)
                                if P4_STAGE < 1.7:
                                    continue
                                eq.wt("vector", s_mm, s_mm.n)
                                eq.wt("vector", s_e2, e2_before)
                                dst = srow[:, 512 * kb:512 * (kb + 1)]
                                if kb == s:
                                    vop(lambda e, d=dst, a=psc[pb][:]:
                                        e.tensor_tensor(out=d, in0=a, in1=m01s[:],
                                                        op=AOP.mult), s_vv)
                                    eq.wt("vector", s_vv, s_vv.n)
                                    vop(lambda e, d=dst: e.tensor_tensor(
                                        out=d, in0=d, in1=madds[:], op=AOP.add),
                                        s_sev)
                                else:
                                    vop(lambda e, d=dst, a=psc[pb][:]:
                                        e.tensor_copy(d, a), s_sev)
                            if P4_STAGE < 2:
                                continue
                            eq.wt("vector", s_sev, s_sev.n)
                            vop(lambda e, w=ext: e.reduce_max(
                                nm[:], srow[:, 0:w], axis=AXX, negate=True),
                                s_vv)
                            eq.wt("vector", s_vv, s_vv.n)
                            vop(lambda e: e.tensor_scalar(
                                out=nms[:], in0=nm[:], scalar1=ISQ, scalar2=None,
                                op0=AOP.mult), s_sm)
                            eq.wt("scalar", s_sm, s_sm.n)
                            eq.wt("scalar", s_esp, s_esp.n)
                            aop(lambda e, w=ext: e.activation(
                                out=ebuf[:, 0:w], in_=srow[:, 0:w], func=AF.Exp,
                                bias=nms[:], scale=isq_ap[:],
                                accum_out=ell4[:]), s_e2)
                            eq.wt("vector", s_e2, s_e2.n)
                            vop(lambda e: e.reciprocal(lnl[:], ell4[:]), s_vv)
                            eq.wt("vector", s_vv, s_vv.n)
                            if P4_STAGE < 3:
                                continue
                            eq.wt("vector", s_tt, s_tt.n)
                            vop(lambda e, w=ext, o=eh[hh][:, 0:ext]:
                                e.tensor_scalar(out=o, in0=ebuf[:, 0:w],
                                                scalar1=lnl[:], scalar2=None,
                                                op0=AOP.mult), s_vv)
                            eq.wt("vector", s_vv, s_vv.n)
                            vop(lambda e, w=ext, o=el[hh][:, 0:ext],
                                h_=eh[hh][:, 0:ext]: e.scalar_tensor_tensor(
                                    out=o, in0=ebuf[:, 0:w], scalar=lnl[:],
                                    in1=h_, op0=AOP.mult, op1=AOP.subtract),
                                s_esp)
                        if P4_STAGE < 4:
                            continue
                        eq.wt("tensor", s_esp, s_esp.n)
                        n_ks = 4 * (s + 1)
                        tte_base = s_tte.n
                        avk_base = s_avk.n
                        for ksu in range(n_ks):
                            pb = ksu % 4
                            eq.wt("tensor", s_tte,
                                  tte_base + 2 * max(0, ksu - 3))
                            if ksu == 0:
                                eq.wt("tensor", s_ao, s_ao.n)
                                eq.wt("tensor", s_qa, kv_t)
                                eq.wt("tensor", s_tte, tte_base)
                                eq.wt("vector", s_av, s_av.n)
                            for hh in range(4):
                                top(lambda e, o=pet[pb][:, 128 * hh:128 * (hh + 1)],
                                    a=eh[hh][:, 128 * ksu:128 * (ksu + 1)]:
                                    e.transpose(o, a, idbs[:]))
                                top(lambda e, o=pet[pb][:, 512 + 128 * hh:
                                                         512 + 128 * (hh + 1)],
                                    a=el[hh][:, 128 * ksu:128 * (ksu + 1)]:
                                    e.transpose(o, a, idbs[:]),
                                    s_tt if hh == 3 else None)
                            eq.wt("vector", s_tt, s_tt.n)
                            eq.wt("vector", s_avk,
                                  avk_base + max(0, ksu - 3))
                            vop(lambda e, o=etsb_h[pb][:], a=pet[pb][:, 0:512]:
                                e.tensor_copy(o, a))
                            vop(lambda e, o=etsb_l[pb][:], a=pet[pb][:, 512:1024]:
                                e.tensor_copy(o, a), s_tte, 2)
                            if P4_STAGE < 5:
                                continue
                            eq.wt("tensor", s_tte, s_tte.n)
                            mm3(pav[:], vsb_h[:, 128 * ksu:128 * (ksu + 1)],
                                vsb_l[:, 128 * ksu:128 * (ksu + 1)],
                                etsb_h[pb][:], etsb_l[pb][:],
                                start=(ksu == 0), stop=(ksu == n_ks - 1),
                                sm=s_av if ksu == n_ks - 1 else s_avk)
                        if P4_STAGE < 6:
                            continue
                        eq.wt("vector", s_av, s_av.n)
                        eq.wt("vector", s_kv, s_kv.n)
                        split(pav[:], aostg_h[:], aostg_l[:], s_ao)
                        eq.wt("gpsimd", s_ao, s_ao.n)
                        d3h = ao_h[512 * kvh:512 * (kvh + 1),
                                   128 * s:128 * (s + 1)].rearrange(
                                       "(h p) f -> p h f", h=4, p=128)
                        d3l = ao_l[512 * kvh:512 * (kvh + 1),
                                   128 * s:128 * (s + 1)].rearrange(
                                       "(h p) f -> p h f", h=4, p=128)
                        dma("gpsimd", d3h, aostg_h[:].rearrange(
                            "p (h f) -> p h f", h=4, f=128), s_kv)
                        dma("gpsimd", d3l, aostg_l[:].rearrange(
                            "p (h f) -> p h f", h=4, f=128), s_kv)
                eq.wt("gpsimd", s_kv, s_kv.n)
                barrier()

        if PHASES >= 5:
            with ExitStack() as p5:
                SB5 = lambda n, s, d: p5.enter_context(nc.sbuf_tensor(n, s, d))
                PS5 = lambda n, s, d: p5.enter_context(nc.psum_tensor(n, s, d))
                wof = [SB5(f"wof{i}", [128, 512], F32) for i in range(2)]
                woh = [SB5(f"woh{i}", [128, 512], BF16) for i in range(2)]
                wol = [SB5(f"wol{i}", [128, 512], BF16) for i in range(2)]
                aot_h = [SB5(f"aoth{i}", [128, 1024], BF16) for i in range(2)]
                aot_l = [SB5(f"aotl{i}", [128, 1024], BF16) for i in range(2)]
                gld = [SB5(f"gld{i}", [128, 1024], F32) for i in range(2)]
                sgst = SB5("sgst", [128, 1024], F32)
                tst = [SB5(f"tst{i}", [128, 512], F32) for i in range(2)]
                po5 = [PS5(f"po5{i}", [128, 1024], F32) for i in range(4)]

                nW5 = [0]
                ntr5 = [0]
                u5_sig = []
                tte5_base = s_tte.n
                for ob in range(8):
                    for ic in range(32):
                        pb = nW5[0] % 2
                        eq.wt("gpsimd", s_ws, max(0, s_ws.n - 1))
                        dma("gpsimd", wof[pb][:],
                            woT[128 * ic:128 * (ic + 1),
                                512 * ob:512 * (ob + 1)], s_w)
                        if nW5[0] >= 2:
                            eq.wt("sync", *u5_sig[nW5[0] - 2])
                        dma("sync", aot_h[pb][:],
                            ao_h[128 * ic:128 * (ic + 1), :], s_mv)
                        dma("sync", aot_l[pb][:],
                            ao_l[128 * ic:128 * (ic + 1), :], s_mv)
                        mvt = s_mv.n
                        eq.wt("vector", s_w, s_w.n)
                        if nW5[0] >= 2:
                            eq.wt("vector", *u5_sig[nW5[0] - 2])
                        split(wof[pb][:], woh[pb][:], wol[pb][:], s_ws)
                        nW5[0] += 1
                        eq.wt("tensor", s_ws, s_ws.n)
                        eq.wt("tensor", s_mv, mvt)
                        if ic == 0:
                            eq.wt("tensor", s_sg, s_sg.n)
                            eq.wt("tensor", s_tte, s_tte.n)
                        for qh2 in range(2):
                            for ot in range(4):
                                last = (qh2 == 1 and ot == 3)
                                mm3(po5[ot][:, 512 * qh2:512 * (qh2 + 1)],
                                    woh[pb][:, 128 * ot:128 * (ot + 1)],
                                    wol[pb][:, 128 * ot:128 * (ot + 1)],
                                    aot_h[pb][:, 512 * qh2:512 * (qh2 + 1)],
                                    aot_l[pb][:, 512 * qh2:512 * (qh2 + 1)],
                                    start=(ic == 0), stop=(ic == 31),
                                    sm=(s_mm if ic == 31 else s_tm)
                                    if last else None,
                                    k=1)
                        u5_sig.append((s_mm, s_mm.n) if ic == 31
                                      else (s_tm, s_tm.n))
                    for ot in range(4):
                        otile = 4 * ob + ot
                        gpb = ot % 2
                        eq.wt("sync", s_sg, max(0, s_sg.n - 1))
                        dma("sync", gld[gpb][:],
                            g_s[128 * otile:128 * (otile + 1), :], s_mv)
                        eq.wt("vector", s_mm, s_mm.n)
                        eq.wt("vector", s_mv, s_mv.n)
                        eq.wt("vector", s_tt, s_tt.n)
                        vop(lambda e, a=po5[ot][:], g=gld[gpb][:]:
                            e.tensor_tensor(out=sgst[:], in0=a, in1=g,
                                            op=AOP.mult), s_sg)
                        eq.wt("tensor", s_sg, s_sg.n)
                        for grp in range(2):
                            ptr5 = po5[ot][:, 512 * grp:512 * (grp + 1)]
                            eq.wt("tensor", s_tte,
                                  tte5_base + max(0, ntr5[0] - 1))
                            for u in range(4):
                                uu = 4 * grp + u
                                top(lambda e, o=ptr5[:, 128 * u:128 * (u + 1)],
                                    a=sgst[:, 128 * uu:128 * (uu + 1)]:
                                    e.transpose(o, a, idfs[:]),
                                    s_tt if u == 3 else None)
                            eq.wt("scalar", s_tt, s_tt.n)
                            eq.wt("scalar", s_o, 16 * max(0, ntr5[0] - 1))
                            aop(lambda e, o=tst[ntr5[0] % 2][:], a=ptr5:
                                e.activation(out=o, in_=a, func=AF.Copy), s_tte)
                            eq.wt("gpsimd", s_tte, s_tte.n)
                            d3 = sgt[512 * grp:512 * (grp + 1),
                                     128 * otile:128 * (otile + 1)].rearrange(
                                         "(s p) f -> p s f", s=4, p=128)
                            dma("gpsimd", d3, tst[ntr5[0] % 2][:].rearrange(
                                "p (s f) -> p s f", s=4, f=128), s_o)
                            ntr5[0] += 1
                eq.wt("gpsimd", s_o, s_o.n)
                barrier()

        if PHASES >= 6:
            with ExitStack() as p6:
                SB6 = lambda n, s, d: p6.enter_context(nc.sbuf_tensor(n, s, d))
                sgl = [SB6(f"sgl{i}", [128, D], F32) for i in range(2)]
                xl6 = [SB6(f"xl6{i}", [128, D], F32) for i in range(2)]
                orow = [SB6(f"orow{i}", [128, D], F32) for i in range(2)]
                o_base = s_o.n
                sg_base = s_sg.n
                for s in range(NS):
                    pb = s % 2
                    eq.wt("sync", s_o, o_base + 16 * max(0, s - 1))
                    dma("sync", sgl[pb][:], sgt[128 * s:128 * (s + 1), :], s_mv)
                    dma("sync", xl6[pb][:], xb[512 * s:512 * s + 128, :], s_mv)
                    eq.wt("vector", s_mv, s_mv.n)
                    eq.wt("vector", s_o, o_base + 16 * max(0, s - 1))
                    vop(lambda e, o=orow[pb][:], a=xl6[pb][:], g=sgl[pb][:]:
                        e.tensor_tensor(out=o, in0=a, in1=g, op=AOP.add), s_sg)
                    eq.wt("gpsimd", s_sg, s_sg.n)
                    dma("gpsimd", out[128 * s:128 * (s + 1), :], orow[pb][:], s_o)
                eq.wt("gpsimd", s_o, s_o.n)
                eq.wt("gpsimd", s_kv, s_kv.n)
                eq.wt("gpsimd", s_g, s_g.n)
                eq.wt("gpsimd", s_st, s_st.n)

        with nc.Block() as blk:
            blk.sync(lambda e: [f(e) for f in eq.q["sync"]])
            blk.tensor(lambda e: [f(e) for f in eq.q["tensor"]])
            blk.vector(lambda e: [f(e) for f in eq.q["vector"]])
            blk.scalar(lambda e: [f(e) for f in eq.q["scalar"]])
            blk.gpsimd(lambda e: [f(e) for f in eq.q["gpsimd"]])

    return nc



def _rope_tables():
    import jax
    import jax.numpy as jnp
    ROPE_THETA = 500000.0
    factor, low_f, high_f, old_ctx = 32.0, 1.0, 4.0, 8192
    inv_freq = 1.0 / ROPE_THETA ** (np.arange(0, HD, 2, dtype=np.float32) / HD)
    low_wl, high_wl = old_ctx / low_f, old_ctx / high_f
    wavelen = 2 * math.pi / inv_freq
    inv_llama = np.where(wavelen > low_wl, inv_freq / factor, inv_freq)
    smooth = (old_ctx / wavelen - low_f) / (high_f - low_f)
    smoothed = (1 - smooth) * inv_llama / factor + smooth * inv_llama
    is_med = ~(wavelen < high_wl) & ~(wavelen > low_wl)
    inv = np.where(is_med, smoothed, inv_llama).astype(np.float32)
    try:
        cpu = jax.devices("cpu")[0]
        with jax.default_device(cpu):
            pos = jnp.arange(T, dtype=jnp.float32)
            freqs = pos[:, None] * jnp.asarray(inv)[None, :]
            emb = jnp.concatenate([freqs, freqs], axis=-1)
            cos = np.asarray(jnp.cos(emb).astype(jnp.float32))
            sin = np.asarray(jnp.sin(emb).astype(jnp.float32))
    except Exception:
        pos = np.arange(T, dtype=np.float32)
        freqs = (pos[:, None] * inv[None, :]).astype(np.float32)
        emb = np.concatenate([freqs, freqs], axis=-1)
        cos = np.cos(emb).astype(np.float32)
        sin = np.sin(emb).astype(np.float32)
    return cos, sin


def kernel(L0_attn_x, L0_attn_Wq, L0_attn_Wk, L0_attn_Wv, L0_attn_Wo,
           L0_ffn_qweight, L0_ffn_scales, L0_ffn_zeros, L0_ffn_bias):
    x = np.asarray(L0_attn_x)
    Wq = np.asarray(L0_attn_Wq)
    Wk = np.asarray(L0_attn_Wk)
    Wv = np.asarray(L0_attn_Wv)
    Wo = np.asarray(L0_attn_Wo)
    qw = np.asarray(L0_ffn_qweight)
    scales = np.asarray(L0_ffn_scales)
    zeros = np.asarray(L0_ffn_zeros)
    bias = np.asarray(L0_ffn_bias)

    cos, sin = _rope_tables()
    wqT = np.ascontiguousarray(Wq.T)
    wkvT = np.ascontiguousarray(np.concatenate([Wk.T, Wv.T], axis=1))
    woT = np.ascontiguousarray(Wo.T)
    scal_c = np.ascontiguousarray(scales[:, 0].reshape(32, 128).T)
    scal_r = np.ascontiguousarray(scales[:, 0].reshape(1, D))
    zero_r = np.ascontiguousarray(zeros[:, 0].reshape(1, D))
    bias_r = np.ascontiguousarray(bias.reshape(1, D))
    idf = np.eye(128, dtype=np.float32)
    idb = np.eye(128, dtype=ml_dtypes.bfloat16)

    nc = build()

    in_maps = []
    for j in range(8):
        b, c = j // 4, j % 4
        u = np.arange(T)
        m = u // 512
        w = (u % 512) // 128
        uu = u % 128
        tau = 512 * m + 128 * ((c + w) % 4) + uu
        xb_ = np.ascontiguousarray(x[b][tau])
        cosL = np.ascontiguousarray(cos[tau].T)
        sinL = np.ascontiguousarray(sin[tau].T)
        qq = np.arange(128)[:, None]
        wcol = np.arange(512)[None, :] // 128
        uucol = np.arange(512)[None, :] % 128
        gq = 128 * c + qq
        gk = 128 * ((c + wcol) % 4) + uucol
        allow = (gk <= gq)
        m01v = allow.astype(np.float32)
        maddv = np.where(allow, 0.0, np.float32(MASKNEG)).astype(np.float32)
        in_maps.append({
            "xb": xb_, "wqT": wqT, "wkvT": wkvT, "woT": woT, "qw": qw,
            "scal_c": scal_c, "scal_r": scal_r, "zero_r": zero_r,
            "bias_r": bias_r, "cosT": cosL, "sinT": sinL,
            "idf": idf, "idb": idb, "m01": np.ascontiguousarray(m01v),
            "madd": np.ascontiguousarray(maddv),
        })

    res = run_bass_kernel_spmd(nc, in_maps, core_ids=list(range(8)))

    outp = np.empty((B, T, D), dtype=np.float32)
    for j in range(8):
        b, c = j // 4, j % 4
        oc = res.results[j]["out"]
        for s in range(NS):
            outp[b, 512 * s + 128 * c:512 * s + 128 * c + 128, :] = \
                oc[128 * s:128 * (s + 1), :]
    globals()["_last_results"] = res
    return outp


# version 06
# speedup vs baseline: 1.1581x; 1.1581x over previous
import math
import numpy as np
import ml_dtypes
import concourse.bass as bass
import concourse.mybir as mybir
from concourse.bass_utils import run_bass_kernel_spmd
from concourse.alu_op_type import AluOpType
from contextlib import ExitStack

F32 = mybir.dt.float32
BF16 = mybir.dt.bfloat16
U32 = mybir.dt.uint32
I32 = mybir.dt.int32
AF = mybir.ActivationFunctionType
AXX = mybir.AxisListType.X
AOP = AluOpType

B, T, D = 2, 4096, 4096
KVD, HD, NH, NKVH = 1024, 128, 32, 8
NT = T // 128
NS = 8
EPS = 1e-5
ISQ = 1.0 / math.sqrt(HD)
MASKNEG = -30000.0

DEBUG_TAPS = ()
PHASES = 6
NT_LIM = None
KVH_LIM = None
S_LIM = None
P4_STAGE = 9
SER_P1 = False
ENGS = ("sync", "tensor", "vector", "scalar", "gpsimd")


class S:
    def __init__(self, ctx, nc, name):
        self.h = ctx.enter_context(nc.semaphore(name))
        self.n = 0


class EQ:
    def __init__(self):
        self.q = {e: [] for e in ENGS}
        self.sig = {e: {} for e in ENGS}

    def wt(self, eng, sem, val):
        if val > 0:
            self.q[eng].append(lambda e, h=sem.h, v=val: e.wait_ge(h, v))

    def op(self, eng, fn, sem=None, k=1):
        if sem is None:
            self.q[eng].append(fn)
        else:
            sem.n += k
            self.sig[eng][sem] = sem.n
            self.q[eng].append(
                lambda e, f=fn, h=sem.h, kk=k: f(e).then_inc(h, kk))


def build():
    nc = bass.Bass("TRN2", target_bir_lowering=False,
                   detect_race_conditions=False)

    xb = nc.dram_tensor("xb", [T, D], F32, kind="ExternalInput").ap()
    wqT = nc.dram_tensor("wqT", [D, D], F32, kind="ExternalInput").ap()
    wkvT = nc.dram_tensor("wkvT", [D, 2 * KVD], F32, kind="ExternalInput").ap()
    woT = nc.dram_tensor("woT", [D, D], F32, kind="ExternalInput").ap()
    qw = nc.dram_tensor("qw", [D // 8, D], I32, kind="ExternalInput").ap()
    scal_c = nc.dram_tensor("scal_c", [128, 32], F32, kind="ExternalInput").ap()
    scal_r = nc.dram_tensor("scal_r", [1, D], F32, kind="ExternalInput").ap()
    zero_r = nc.dram_tensor("zero_r", [1, D], F32, kind="ExternalInput").ap()
    bias_r = nc.dram_tensor("bias_r", [1, D], F32, kind="ExternalInput").ap()
    cosT = nc.dram_tensor("cosT", [128, T], F32, kind="ExternalInput").ap()
    sinT = nc.dram_tensor("sinT", [128, T], F32, kind="ExternalInput").ap()
    idf = nc.dram_tensor("idf", [128, 128], F32, kind="ExternalInput").ap()
    idb = nc.dram_tensor("idb", [128, 128], BF16, kind="ExternalInput").ap()
    m01 = nc.dram_tensor("m01", [128, 512], F32, kind="ExternalInput").ap()
    madd = nc.dram_tensor("madd", [128, 512], F32, kind="ExternalInput").ap()

    out = nc.dram_tensor("out", [NS * 128, D], F32, kind="ExternalOutput").ap()

    def scratch(name, shape, dt):
        kind = "ExternalOutput" if name in DEBUG_TAPS else "Internal"
        return nc.dram_tensor(name, shape, dt, kind=kind).ap()

    xn_h = scratch("xn_h", [D, T], BF16)
    xn_l = scratch("xn_l", [D, T], BF16)
    xnp_h = scratch("xnp_h", [D, 1024], BF16)
    xnp_l = scratch("xnp_l", [D, 1024], BF16)
    q_h = scratch("q_h", [D, 1024], BF16)
    q_l = scratch("q_l", [D, 1024], BF16)
    k_h = scratch("k_h", [KVD, T], BF16)
    k_l = scratch("k_l", [KVD, T], BF16)
    v_h = scratch("v_h", [T, KVD], BF16)
    v_l = scratch("v_l", [T, KVD], BF16)
    ao_h = scratch("ao_h", [D, 1024], BF16)
    ao_l = scratch("ao_l", [D, 1024], BF16)
    g_s = scratch("g_s", [D, 1024], F32)
    sgt = scratch("sgt", [1024, D], F32)
    dbg1 = scratch("dbg1", [128, D], F32)
    dbg2 = scratch("dbg2", [128, D], BF16)
    dbg3 = scratch("dbg3", [128, D], BF16)
    dbg4 = scratch("dbg4", [128, D], F32)
    dbg5 = scratch("dbg5", [128, 8], F32)

    eq = EQ()
    ctx = ExitStack()
    with ctx:
        sem = lambda name: S(ctx, nc, name)
        SBg = lambda n, s, d: ctx.enter_context(nc.sbuf_tensor(n, s, d))

        idfs = SBg("idfs", [128, 128], F32)
        idbs = SBg("idbs", [128, 128], BF16)
        m01s = SBg("m01s", [128, 512], F32)
        madds = SBg("madds", [128, 512], F32)
        scalcs = SBg("scalcs", [128, 32], F32)
        l5 = SBg("l5", [5, D], BF16)
        r5 = SBg("r5", [5, 1024], BF16)
        scol = SBg("scol", [128, 8], F32)
        eps_ap = SBg("eps_ap", [128, 1], F32)
        isq_ap = SBg("isq_ap", [128, 1], F32)
        dmy1 = SBg("dmy1", [128, 1], F32)
        dmy2 = SBg("dmy2", [128, 1], F32)

        s_c0 = sem("s_c0"); s_ph = sem("s_ph"); s_x = sem("s_x")
        s_sq = sem("s_sq"); s_var = sem("s_var"); s_rs = sem("s_rs")
        s_bln = sem("s_bln"); s_apl = sem("s_apl"); s_sp = sem("s_sp")
        s_tr = sem("s_tr"); s_ev = sem("s_ev"); s_st = sem("s_st")
        s_l5 = sem("s_l5"); s_w = sem("s_w"); s_ws = sem("s_ws")
        s_mv = sem("s_mv"); s_mm = sem("s_mm"); s_tm = sem("s_tm")
        s_rope = sem("s_rope"); s_kv = sem("s_kv"); s_u = sem("s_u")
        s_g = sem("s_g"); s_qa = sem("s_qa"); s_sev = sem("s_sev")
        s_sm = sem("s_sm"); s_e1 = sem("s_e1"); s_e2 = sem("s_e2")
        s_esp = sem("s_esp"); s_tt = sem("s_tt"); s_tte = sem("s_tte")
        s_av = sem("s_av"); s_avk = sem("s_avk"); s_ao = sem("s_ao")
        s_sg = sem("s_sg"); s_o = sem("s_o"); s_vv = sem("s_vv")
        s_wsb = sem("s_wsb")

        phase_no = [0]

        def vbounce(sm):
            eq.wt("scalar", sm, sm.n)
            aop(lambda e: e.activation(out=dmy2[:], in_=dmy1[:],
                                       func=AF.Copy), s_wsb)
            eq.wt("vector", s_wsb, s_wsb.n)

        def barrier():
            snap = {e2: [(s, s.n) for s in eq.sig[e2]] for e2 in ENGS}
            for e1 in ENGS:
                for e2 in ENGS:
                    if e1 == e2:
                        continue
                    for s, v in snap[e2]:
                        eq.wt(e1, s, v)

        def dma(eng, o, i_, sm):
            eq.op(eng, lambda e, oo=o, ii=i_: e.dma_start(out=oo, in_=ii),
                  sm, 16)

        def split(src, hi, lo, sm=None):
            eq.op("vector", lambda e, a=src, b=hi: e.tensor_copy(b, a), s_vv)
            eq.wt("vector", s_vv, s_vv.n)
            eq.op("vector",
                  lambda e, a=src, b=hi, c=lo: e.scalar_tensor_tensor(
                      out=c, in0=a, scalar=1.0, in1=b, op0=AOP.mult,
                      op1=AOP.subtract), sm)

        def mm3(o, lh, ll, rh, rl, start, stop, sm=None, k=1):
            eq.op("tensor", lambda e, oo=o, a=lh, b=rh, st=start:
                  e.matmul(oo, a, b, start=st, stop=False))
            eq.op("tensor", lambda e, oo=o, a=lh, b=rl:
                  e.matmul(oo, a, b, start=False, stop=False))
            eq.op("tensor", lambda e, oo=o, a=ll, b=rh, sp=stop:
                  e.matmul(oo, a, b, start=False, stop=sp), sm, k)

        def vop(fn, sm=None, k=1):
            eq.op("vector", fn, sm, k)

        def aop(fn, sm=None, k=1):
            eq.op("scalar", fn, sm, k)

        def top(fn, sm=None, k=1):
            eq.op("tensor", fn, sm, k)

        p0 = ExitStack()
        SB0 = lambda n, s, d: p0.enter_context(nc.sbuf_tensor(n, s, d))
        rowf = SB0("rowf", [1, D], F32)
        rowf2 = SB0("rowf2", [1, D], F32)
        rowf3 = SB0("rowf3", [1, D], F32)
        rowbh = SB0("rowbh", [1, D], BF16)
        rowbl = SB0("rowbl", [1, D], BF16)
        rowch = SB0("rowch", [1, D], BF16)
        rowcl = SB0("rowcl", [1, D], BF16)
        row1 = SB0("row1", [1, 1024], BF16)
        for hb, sb_ in ((idf, idfs), (idb, idbs), (m01, m01s), (madd, madds),
                        (scal_c, scalcs)):
            dma("sync", sb_[:], hb[:], s_c0)
        dma("sync", rowf[:], scal_r[:], s_c0)
        dma("sync", rowf2[:], zero_r[:], s_c0)
        dma("sync", rowf3[:], bias_r[:], s_c0)
        eq.wt("vector", s_c0, s_c0.n)
        vop(lambda e: e.reciprocal(rowf[:], rowf[:]), s_vv)
        eq.wt("vector", s_vv, s_vv.n)
        vop(lambda e: e.scalar_tensor_tensor(
            out=rowf2[:], in0=rowf2[:], scalar=-1.0, in1=rowf[:],
            op0=AOP.mult, op1=AOP.mult), s_vv)
        eq.wt("vector", s_vv, s_vv.n)
        vop(lambda e: e.tensor_tensor(out=rowf3[:], in0=rowf3[:], in1=rowf[:],
                                      op=AOP.mult), s_vv)
        eq.wt("vector", s_vv, s_vv.n)
        vop(lambda e: e.tensor_copy(rowbh[:], rowf2[:]), s_vv)
        eq.wt("vector", s_vv, s_vv.n)
        vop(lambda e: e.scalar_tensor_tensor(
            out=rowbl[:], in0=rowf2[:], scalar=1.0, in1=rowbh[:],
            op0=AOP.mult, op1=AOP.subtract), s_vv)
        eq.wt("vector", s_vv, s_vv.n)
        vop(lambda e: e.tensor_copy(rowch[:], rowf3[:]), s_vv)
        eq.wt("vector", s_vv, s_vv.n)
        vop(lambda e: e.scalar_tensor_tensor(
            out=rowcl[:], in0=rowf3[:], scalar=1.0, in1=rowch[:],
            op0=AOP.mult, op1=AOP.subtract), s_vv)
        eq.wt("vector", s_vv, s_vv.n)
        vop(lambda e: e.memset(row1[:], 1.0))
        vop(lambda e: e.memset(eps_ap[:], EPS))
        vop(lambda e: e.memset(isq_ap[:], ISQ), s_l5)
        eq.wt("sync", s_l5, s_l5.n)
        dma("sync", l5[0:1, :], rowbh[:], s_l5)
        dma("sync", l5[1:2, :], rowbh[:], s_l5)
        dma("sync", l5[2:3, :], rowbl[:], s_l5)
        dma("sync", l5[3:4, :], rowch[:], s_l5)
        dma("sync", l5[4:5, :], rowcl[:], s_l5)
        dma("sync", r5[3:4, :], row1[:], s_l5)
        dma("sync", r5[4:5, :], row1[:], s_l5)
        eq.wt("sync", s_c0, s_c0.n)
        eq.wt("sync", s_l5, s_l5.n)
        p0.close()
        barrier()

        with ExitStack() as p1:
            SB1 = lambda n, s, d: p1.enter_context(nc.sbuf_tensor(n, s, d))
            PS1 = lambda n, s, d: p1.enter_context(nc.psum_tensor(n, s, d))
            xtile = [SB1(f"xt{i}", [128, D], F32) for i in range(2)]
            xn32 = [SB1(f"xn32{i}", [128, D], F32) for i in range(2)]
            xnh = [SB1(f"xnh{i}", [128, D], BF16) for i in range(2)]
            xnl = [SB1(f"xnl{i}", [128, D], BF16) for i in range(2)]
            stg = [SB1(f"stg{i}", [128, 1024], BF16) for i in range(4)]
            st1 = SB1("st1", [128, 1], F32)
            st2 = SB1("st2", [128, 1], F32)
            st3 = SB1("st3", [128, 1], F32)
            st4 = SB1("st4", [128, 1], F32)
            st5 = SB1("st5", [128, 1], F32)
            st6 = SB1("st6", [128, 1], F32)
            srow_f = SB1("srow_f", [1, 1024], F32)
            srowbh = SB1("srowbh", [1, 1024], BF16)
            srowbl = SB1("srowbl", [1, 1024], BF16)
            trb = [PS1(f"trb{i}", [128, 1024], BF16) for i in range(4)]
            spsum = PS1("spsum", [1, 1024], F32)

            ngrp = [0]
            tr_after_tile = {}

            def do_groups(src_hi, src_lo, k, perm):
                s_own = k // 4
                for src, dsth in ((src_hi, xnp_h if perm else xn_h),
                                  (src_lo, xnp_l if perm else xn_l)):
                    for g in range(4):
                        n = ngrp[0]
                        bank = trb[n % 4]
                        eq.wt("tensor", s_st, 16 * max(0, n - 3))
                        for u in range(8):
                            cj = 8 * g + u
                            if perm:
                                c2, j2 = cj // 8, cj % 8
                                ap_in = src[:, 1024 * c2 + j2:
                                            1024 * c2 + j2 + 1017:8]
                            else:
                                ap_in = src[:, 128 * cj:128 * cj + 128]
                            last = (u == 7)
                            top(lambda e, o=bank[:, 128 * u:128 * (u + 1)],
                                a=ap_in: e.transpose(o, a, idbs[:]),
                                s_tr if last else None)
                        eq.wt("scalar", s_tr, s_tr.n)
                        eq.wt("scalar", s_st, 16 * max(0, n - 3))
                        stgb = stg[n % 4]
                        aop(lambda e, o=stgb[:], a=bank[:]:
                            e.activation(out=o, in_=a, func=AF.Copy), s_ev)
                        eq.wt("gpsimd", s_ev, s_ev.n)
                        if perm:
                            dst_ap = dsth[1024 * g:1024 * (g + 1),
                                          128 * s_own:128 * s_own + 128]
                        else:
                            dst_ap = dsth[1024 * g:1024 * (g + 1),
                                          128 * k:128 * k + 128]
                        d3 = dst_ap.rearrange("(s p) f -> p s f", s=8, p=128)
                        i3 = stgb[:].rearrange("p (s f) -> p s f", s=8, f=128)
                        dma("gpsimd", d3, i3, s_st)
                        ngrp[0] += 1

            for k in range(NT):
                p = k % 2
                eq.wt("sync", s_apl, max(0, k - 1))
                dma("sync", xtile[p][:], xb[128 * k:128 * (k + 1), :], s_x)
                eq.wt("vector", s_x, 16 * (k + 1))
                vop(lambda e, a=xtile[p][:]: e.reduce_sum(
                    st1[:], a, axis=AXX, negate=True), s_vv)
                eq.wt("vector", s_vv, s_vv.n)
                eq.wt("scalar", s_x, 16 * (k + 1))
                eq.wt("scalar", s_sp, max(0, k - 1))
                eq.wt("scalar", s_var, k)
                aop(lambda e, o=xn32[p][:], a=xtile[p][:]: e.activation(
                    out=o, in_=a, func=AF.Square, accum_out=st2[:]), s_sq)
                eq.wt("vector", s_sq, s_sq.n)
                vop(lambda e: e.tensor_scalar(out=st3[:], in0=st1[:],
                                              scalar1=1.0 / D, scalar2=None,
                                              op0=AOP.mult), s_vv)
                eq.wt("vector", s_vv, s_vv.n)
                vop(lambda e: e.tensor_tensor(out=st4[:], in0=st3[:],
                                              in1=st3[:], op=AOP.mult), s_vv)
                eq.wt("vector", s_vv, s_vv.n)
                vop(lambda e: e.scalar_tensor_tensor(
                    out=st4[:], in0=st2[:], scalar=1.0 / D, in1=st4[:],
                    op0=AOP.mult, op1=AOP.subtract), s_var)
                eq.wt("scalar", s_var, s_var.n)
                if k == 0:
                    eq.wt("scalar", s_l5, s_l5.n)
                aop(lambda e: e.activation(out=st5[:], in_=st4[:],
                                           func=AF.Sqrt, bias=eps_ap[:],
                                           scale=1.0), s_rs)
                eq.wt("vector", s_rs, s_rs.n)
                vop(lambda e: e.reciprocal(st5[:], st5[:]), s_vv)
                eq.wt("vector", s_vv, s_vv.n)
                vop(lambda e: e.tensor_tensor(out=st6[:], in0=st3[:],
                                              in1=st5[:], op=AOP.mult), s_bln)
                eq.wt("vector", s_bln, s_bln.n)
                eq.wt("vector", s_sq, s_sq.n)
                vop(lambda e, o=xn32[p][:], a=xtile[p][:]: e.tensor_scalar(
                    out=o, in0=a, scalar1=st5[:], scalar2=st6[:],
                    op0=AOP.mult, op1=AOP.add), s_apl)
                eq.wt("vector", s_apl, s_apl.n)
                if k % 4 == 0:
                    vop(lambda e, a=xn32[p][:], c=scol[:, k // 4:k // 4 + 1]:
                        e.reduce_sum(c, a, axis=AXX))
                eq.wt("vector", s_tr, tr_after_tile.get(k - 2, 0))
                split(xn32[p][:], xnh[p][:], xnl[p][:], s_sp)
                if k == 0 and "dbg1" in DEBUG_TAPS:
                    eq.wt("gpsimd", s_sp, s_sp.n)
                    dma("gpsimd", dbg1[:], xn32[p][:], s_st)
                    dma("gpsimd", dbg2[:], xnh[p][:], s_st)
                    dma("gpsimd", dbg3[:], xnl[p][:], s_st)
                    dma("gpsimd", dbg4[:], xtile[p][:], s_st)
                    for di_, stx in enumerate((st1, st2, st3, st4, st5, st6)):
                        vop(lambda e, oo=scol[:, di_:di_ + 1], ii=stx[:]:
                            e.tensor_copy(oo, ii), s_sp)
                    eq.wt("gpsimd", s_sp, s_sp.n)
                    dma("gpsimd", dbg5[:], scol[:], s_st)
                eq.wt("tensor", s_sp, s_sp.n)
                do_groups(xnh[p], xnl[p], k, perm=False)
                if k % 4 == 0:
                    do_groups(xnh[p], xnl[p], k, perm=True)
                tr_after_tile[k] = s_tr.n
                if SER_P1:
                    eq.wt("gpsimd", s_st, s_st.n)
                    barrier()
                if NT_LIM is not None and k + 1 >= NT_LIM:
                    break
            eq.wt("tensor", s_sp, s_sp.n)
            for s in range(8):
                top(lambda e, o=spsum[0:1, 128 * s:128 * (s + 1)],
                    a=scol[:, s:s + 1]: e.transpose(o, a, idfs[:]),
                    s_tr if s == 7 else None)
            eq.wt("vector", s_tr, s_tr.n)
            vop(lambda e: e.tensor_copy(srow_f[:], spsum[0:1, :]), s_vv)
            eq.wt("vector", s_vv, s_vv.n)
            vop(lambda e: e.tensor_copy(srowbh[:], srow_f[:]), s_vv)
            eq.wt("vector", s_vv, s_vv.n)
            vop(lambda e: e.scalar_tensor_tensor(
                out=srowbl[:], in0=srow_f[:], scalar=1.0,
                in1=srowbh[:], op0=AOP.mult, op1=AOP.subtract), s_l5)
            eq.wt("sync", s_l5, s_l5.n)
            dma("sync", r5[0:1, 0:1024], srowbh[:], s_l5)
            dma("sync", r5[1:2, 0:1024], srowbl[:], s_l5)
            dma("sync", r5[2:3, 0:1024], srowbh[:], s_l5)
            eq.wt("sync", s_l5, s_l5.n)
            eq.wt("gpsimd", s_st, s_st.n)
            barrier()

        if PHASES >= 2:
            with ExitStack() as p2:
                SB2 = lambda n, s, d: p2.enter_context(nc.sbuf_tensor(n, s, d))
                PS2 = lambda n, s, d: p2.enter_context(nc.psum_tensor(n, s, d))
                coss = SB2("coss", [128, T], F32)
                sins = SB2("sins", [128, T], F32)
                xnb_h = SB2("xnb_h", [128, 32 * 512], BF16)
                xnb_l = SB2("xnb_l", [128, 32 * 512], BF16)
                wf = [SB2(f"wf{i}", [128, 1024], F32) for i in range(2)]
                whb = [SB2(f"whb{i}", [128, 1024], BF16) for i in range(2)]
                wlb = [SB2(f"wlb{i}", [128, 1024], BF16) for i in range(2)]
                rt1 = SB2("rt1", [128, 512], F32)
                rt2 = SB2("rt2", [128, 512], F32)
                rt3 = SB2("rt3", [128, 512], F32)
                sth = SB2("sth", [128, 512], BF16)
                stl = SB2("stl", [128, 512], BF16)
                pk = [PS2(f"pk{i}", [128, 512], F32) for i in range(8)]
                ptr = pk[0][:].bitcast(BF16)

                dma("sync", coss[:], cosT[:], s_c0)
                dma("sync", sins[:], sinT[:], s_c0)
                eq.wt("vector", s_c0, s_c0.n)

                nW = [0]
                u_sig = []

                def wload(src_ap):
                    pb = nW[0] % 2
                    eq.wt("gpsimd", s_ws, max(0, nW[0] - 1))
                    dma("gpsimd", wf[pb][:], src_ap, s_w)
                    eq.wt("vector", s_w, s_w.n)
                    if nW[0] >= 2:
                        eq.wt("vector", *u_sig[nW[0] - 2])
                    split(wf[pb][:], whb[pb][:], wlb[pb][:], s_ws)
                    nW[0] += 1
                    return whb[pb], wlb[pb]

                def rope_ops(psrc, cos_ap, sin_ap, threed):
                    if threed:
                        o1 = rt1[:, 0:512].rearrange("p (u f) -> p u f", u=4, f=128)
                        i0 = psrc.rearrange("p (u f) -> p u f", u=4, f=128)
                        o3 = rt3[:, 0:512].rearrange("p (u f) -> p u f", u=4, f=128)
                        i2 = rt2[:, 0:512].rearrange("p (u f) -> p u f", u=4, f=128)
                    else:
                        o1, i0, o3, i2 = rt1[:, 0:512], psrc, rt3[:, 0:512], \
                            rt2[:, 0:512]
                    vop(lambda e: e.tensor_tensor(out=o1, in0=i0, in1=cos_ap,
                                                  op=AOP.mult))
                    vop(lambda e, a=psrc: e.tensor_scalar(
                        out=rt2[0:64, 0:512], in0=a[64:128, :], scalar1=-1.0,
                        scalar2=None, op0=AOP.mult))
                    vop(lambda e, a=psrc: e.tensor_copy(rt2[64:128, 0:512],
                                                        a[0:64, :]), s_vv)
                    eq.wt("vector", s_vv, s_vv.n)
                    vop(lambda e: e.tensor_tensor(out=o3, in0=i2, in1=sin_ap,
                                                  op=AOP.mult), s_vv)
                    eq.wt("vector", s_vv, s_vv.n)
                    vop(lambda e: e.tensor_tensor(out=rt1[:, 0:512],
                                                  in0=rt1[:, 0:512],
                                                  in1=rt3[:, 0:512], op=AOP.add),
                        s_vv)
                    eq.wt("vector", s_vv, s_vv.n)

                def stage_store(src32, dh_ap, dl_ap):
                    eq.wt("vector", s_kv, s_kv.n)
                    split(src32, sth[:, 0:512], stl[:, 0:512], s_rope)
                    eq.wt("gpsimd", s_rope, s_rope.n)
                    dma("gpsimd", dh_ap, sth[:, 0:512], s_kv)
                    dma("gpsimd", dl_ap, stl[:, 0:512], s_kv)

                for tb in range(8):
                    if u_sig:
                        eq.wt("sync", *u_sig[-1])
                    for ic in range(32):
                        dma("sync", xnb_h[:, 512 * ic:512 * (ic + 1)],
                            xn_h[128 * ic:128 * (ic + 1), 512 * tb:512 * (tb + 1)],
                            s_mv)
                        dma("sync", xnb_l[:, 512 * ic:512 * (ic + 1)],
                            xn_l[128 * ic:128 * (ic + 1), 512 * tb:512 * (tb + 1)],
                            s_mv)
                    mv_target = s_mv.n
                    for ob in range(2):
                        for ic in range(32):
                            wh, wl = wload(wkvT[128 * ic:128 * (ic + 1),
                                                1024 * ob:1024 * (ob + 1)])
                            eq.wt("tensor", s_ws, s_ws.n)
                            if ic == 0:
                                eq.wt("tensor", s_mv, mv_target)
                                eq.wt("tensor", s_rope, s_rope.n)
                                eq.wt("tensor", s_ev, s_ev.n)
                                eq.wt("tensor", s_kv, s_kv.n)
                            xh = xnb_h[:, 512 * ic:512 * (ic + 1)]
                            xl = xnb_l[:, 512 * ic:512 * (ic + 1)]
                            for ot in range(8):
                                fin = s_mm if ic == 31 else s_tm
                                mm3(pk[ot][:], wh[:, 128 * ot:128 * (ot + 1)],
                                    wl[:, 128 * ot:128 * (ot + 1)], xh, xl,
                                    start=(ic == 0), stop=(ic == 31),
                                    sm=fin if ot == 7 else None)
                            u_sig.append((s_mm, s_mm.n) if ic == 31
                                         else (s_tm, s_tm.n))
                        eq.wt("vector", s_mm, s_mm.n)
                        if ob == 0:
                            for ot in range(8):
                                rope_ops(pk[ot][:],
                                         coss[:, 512 * tb:512 * (tb + 1)],
                                         sins[:, 512 * tb:512 * (tb + 1)], False)
                                stage_store(
                                    rt1[:, 0:512],
                                    k_h[128 * ot:128 * (ot + 1),
                                        512 * tb:512 * (tb + 1)],
                                    k_l[128 * ot:128 * (ot + 1),
                                        512 * tb:512 * (tb + 1)])
                        else:
                            for ot in range(8):
                                eq.wt("vector", s_kv, s_kv.n)
                                split(pk[ot][:], sth[:, 0:512], stl[:, 0:512],
                                      s_rope)
                                eq.wt("tensor", s_rope, s_rope.n)
                                eq.wt("tensor", s_ev, s_ev.n)
                                for half, stsrc in ((0, sth), (1, stl)):
                                    for u in range(4):
                                        top(lambda e,
                                            o=ptr[:, 128 * (4 * half + u):
                                                  128 * (4 * half + u) + 128],
                                            a=stsrc[:, 128 * u:128 * u + 128]:
                                            e.transpose(o, a, idbs[:]),
                                            s_tr if (half, u) == (1, 3) else None)
                                eq.wt("scalar", s_tr, s_tr.n)
                                eq.wt("scalar", s_kv, s_kv.n)
                                aop(lambda e: e.activation(
                                    out=sth[:, 0:512], in_=ptr[:, 0:512],
                                    func=AF.Copy), s_ev)
                                aop(lambda e: e.activation(
                                    out=stl[:, 0:512], in_=ptr[:, 512:1024],
                                    func=AF.Copy), s_ev)
                                eq.wt("gpsimd", s_ev, s_ev.n)
                                d3h = v_h[512 * tb:512 * (tb + 1),
                                          128 * ot:128 * (ot + 1)].rearrange(
                                              "(s p) f -> p s f", s=4, p=128)
                                d3l = v_l[512 * tb:512 * (tb + 1),
                                          128 * ot:128 * (ot + 1)].rearrange(
                                              "(s p) f -> p s f", s=4, p=128)
                                dma("gpsimd", d3h, sth[:, 0:512].rearrange(
                                    "p (s f) -> p s f", s=4, f=128), s_kv)
                                dma("gpsimd", d3l, stl[:, 0:512].rearrange(
                                    "p (s f) -> p s f", s=4, f=128), s_kv)


                for ob in range(4):
                    for th in range(2):
                        eq.wt("sync", *u_sig[-1])
                        eq.wt("scalar", *u_sig[-1])
                        for ic in range(32):
                            src_h = xn_h[128 * ic:128 * (ic + 1),
                                         2048 * th:2048 * (th + 1)].rearrange(
                                             "p (u w) -> p u w", u=4, w=512)[
                                                 :, :, 0:128]
                            src_l = xn_l[128 * ic:128 * (ic + 1),
                                         2048 * th:2048 * (th + 1)].rearrange(
                                             "p (u w) -> p u w", u=4, w=512)[
                                                 :, :, 0:128]
                            dma("sync",
                                xnb_h[:, 512 * ic:512 * (ic + 1)].rearrange(
                                    "p (u f) -> p u f", u=4, f=128), src_h, s_mv)
                            dma("sync",
                                xnb_l[:, 512 * ic:512 * (ic + 1)].rearrange(
                                    "p (u f) -> p u f", u=4, f=128), src_l, s_mv)
                        mv_target = s_mv.n
                        for ic in range(32):
                            wh, wl = wload(wqT[128 * ic:128 * (ic + 1),
                                               1024 * ob:1024 * (ob + 1)])
                            eq.wt("tensor", s_ws, s_ws.n)
                            if ic == 0:
                                eq.wt("tensor", s_mv, mv_target)
                                eq.wt("tensor", s_rope, s_rope.n)
                                eq.wt("tensor", s_ev, s_ev.n)
                                eq.wt("tensor", s_kv, s_kv.n)
                            xh = xnb_h[:, 512 * ic:512 * (ic + 1)]
                            xl = xnb_l[:, 512 * ic:512 * (ic + 1)]
                            for ot in range(8):
                                fin = s_mm if ic == 31 else s_tm
                                mm3(pk[ot][:], wh[:, 128 * ot:128 * (ot + 1)],
                                    wl[:, 128 * ot:128 * (ot + 1)], xh, xl,
                                    start=(ic == 0), stop=(ic == 31),
                                    sm=fin if ot == 7 else None)
                            u_sig.append((s_mm, s_mm.n) if ic == 31
                                         else (s_tm, s_tm.n))
                        eq.wt("vector", s_mm, s_mm.n)
                        cos3 = coss.rearrange("p (m w) -> p m w", m=8, w=512)[
                            :, 4 * th:4 * th + 4, 0:128]
                        sin3 = sins.rearrange("p (m w) -> p m w", m=8, w=512)[
                            :, 4 * th:4 * th + 4, 0:128]
                        for ot in range(8):
                            head = 8 * ob + ot
                            rope_ops(pk[ot][:], cos3, sin3, True)
                            stage_store(rt1[:, 0:512],
                                        q_h[128 * head:128 * (head + 1),
                                            512 * th:512 * (th + 1)],
                                        q_l[128 * head:128 * (head + 1),
                                            512 * th:512 * (th + 1)])
                barrier()

        if PHASES >= 3:
            with ExitStack() as p3:
                SB3 = lambda n, s, d: p3.enter_context(nc.sbuf_tensor(n, s, d))
                PS3 = lambda n, s, d: p3.enter_context(nc.psum_tensor(n, s, d))
                qwc = [SB3(f"qwc{i}", [128, D], U32) for i in range(4)]
                utmp = SB3("utmp", [128, 512], U32)
                ub = [SB3(f"ub{i}", [128, 512], BF16) for i in range(2)]
                xph = [SB3(f"xph{i}", [128, 1024], BF16) for i in range(2)]
                xpl = [SB3(f"xpl{i}", [128, 1024], BF16) for i in range(2)]
                gst = [SB3(f"gst{i}", [128, 1024], F32) for i in range(2)]
                pf = [PS3(f"pf{i}", [128, 1024], F32) for i in range(4)]

                for c2 in range(4):
                    dma("sync", qwc[c2][:],
                        qw[128 * c2:128 * (c2 + 1), :].bitcast(U32), s_c0)
                c0t = s_c0.n
                eq.wt("vector", s_c0, c0t)
                eq.wt("vector", s_l5, s_l5.n)
                nu = [0]
                nsl = [0]
                e1_base = s_e1.n
                for ob in range(8):
                    for cj in range(32):
                        c2, j2 = cj // 8, cj % 8
                        pb = nu[0] % 2
                        eq.wt("sync", s_tm, max(0, s_tm.n - 1))
                        dma("sync", xph[pb][:],
                            xnp_h[128 * cj:128 * (cj + 1), :], s_mv)
                        dma("sync", xpl[pb][:],
                            xnp_l[128 * cj:128 * (cj + 1), :], s_mv)
                        mvt = s_mv.n
                        eq.wt("vector", s_tm, max(0, s_tm.n - 1))
                        vop(lambda e, a=qwc[c2][:, 512 * ob:512 * (ob + 1)],
                            sh=4 * j2: e.tensor_scalar(
                                out=utmp[:], in0=a, scalar1=sh, scalar2=15,
                                op0=AOP.logical_shift_right, op1=AOP.bitwise_and),
                            s_vv)
                        eq.wt("vector", s_vv, s_vv.n)
                        vop(lambda e, o=ub[pb][:]: e.tensor_copy(o, utmp[:]), s_u)
                        eq.wt("tensor", s_u, s_u.n)
                        eq.wt("tensor", s_mv, mvt)
                        if cj == 0:
                            eq.wt("tensor", s_e1, e1_base + 4 * ob)
                        for ot in range(4):
                            for th2 in range(2):
                                po = pf[ot][:, 512 * th2:512 * (th2 + 1)]
                                eq.op("tensor", lambda e, oo=po,
                                      a=ub[pb][:, 128 * ot:128 * (ot + 1)],
                                      b=xph[pb][:, 512 * th2:512 * (th2 + 1)],
                                      st=(cj == 0): e.matmul(oo, a, b, start=st,
                                                             stop=False))
                                last = (ot == 3 and th2 == 1)
                                eq.op("tensor", lambda e, oo=po,
                                      a=ub[pb][:, 128 * ot:128 * (ot + 1)],
                                      b=xpl[pb][:, 512 * th2:512 * (th2 + 1)]:
                                      e.matmul(oo, a, b, start=False, stop=False),
                                      s_tm if last else None)
                        nu[0] += 1
                    for ot in range(4):
                        ocol = 512 * ob + 128 * ot
                        for th2 in range(2):
                            po = pf[ot][:, 512 * th2:512 * (th2 + 1)]
                            last = (ot == 3 and th2 == 1)
                            eq.op("tensor", lambda e, oo=po,
                                  a=l5[:, ocol:ocol + 128],
                                  b=r5[:, 512 * th2:512 * (th2 + 1)]:
                                  e.matmul(oo, a, b, start=False, stop=True),
                                  s_mm if last else None)
                    eq.wt("scalar", s_mm, s_mm.n)
                    for ot in range(4):
                        otile = 4 * ob + ot
                        gpb = nsl[0] % 2
                        eq.wt("scalar", s_g, 16 * max(0, nsl[0] - 1))
                        aop(lambda e, o=gst[gpb][:], a=pf[ot][:],
                            sc=scalcs[:, otile:otile + 1]: e.activation(
                                out=o, in_=a, func=AF.Silu, scale=sc, bias=0.0),
                            s_e1)
                        eq.wt("gpsimd", s_e1, s_e1.n)
                        dma("gpsimd", g_s[128 * otile:128 * (otile + 1), :],
                            gst[gpb][:], s_g)
                        nsl[0] += 1
                eq.wt("gpsimd", s_g, s_g.n)
                barrier()

        if PHASES >= 4:
            with ExitStack() as p4:
                SB4 = lambda n, s, d: p4.enter_context(nc.sbuf_tensor(n, s, d))
                PS4 = lambda n, s, d: p4.enter_context(nc.psum_tensor(n, s, d))
                ksb_h = SB4("ksb_h", [128, T], BF16)
                ksb_l = SB4("ksb_l", [128, T], BF16)
                vsb_h = SB4("vsb_h", [128, T], BF16)
                vsb_l = SB4("vsb_l", [128, T], BF16)
                qsb_h = [SB4(f"qsbh{i}", [128, 128], BF16) for i in range(4)]
                qsb_l = [SB4(f"qsbl{i}", [128, 128], BF16) for i in range(4)]
                srow = SB4("srow", [128, T], F32)
                ebuf = SB4("ebuf", [128, T], F32)
                eh = [SB4(f"eh{i}", [128, T], BF16) for i in range(4)]
                el = [SB4(f"el{i}", [128, T], BF16) for i in range(4)]
                etsb_h = [SB4(f"etsbh{i}", [128, 512], BF16) for i in range(4)]
                etsb_l = [SB4(f"etsbl{i}", [128, 512], BF16) for i in range(4)]
                aostg_h = SB4("aostg_h", [128, 512], BF16)
                aostg_l = SB4("aostg_l", [128, 512], BF16)
                nm = SB4("nm", [128, 1], F32)
                nms = SB4("nms", [128, 1], F32)
                ell4 = SB4("ell4", [128, 1], F32)
                lnl = SB4("lnl", [128, 1], F32)
                b2 = SB4("b2", [128, 1], F32)
                psc = [PS4(f"psc{i}", [128, 512], F32) for i in range(2)]
                pet = [PS4(f"pet{i}", [128, 1024], BF16) for i in range(4)]
                pav = PS4("pav", [128, 512], F32)

                nkb = [0]
                for kvh in range(KVH_LIM or NKVH):
                    eq.wt("sync", s_av, s_av.n)
                    dma("sync", ksb_h[:], k_h[128 * kvh:128 * (kvh + 1), :], s_qa)
                    dma("sync", ksb_l[:], k_l[128 * kvh:128 * (kvh + 1), :], s_qa)
                    for kt in range(32):
                        dma("sync", vsb_h[:, 128 * kt:128 * (kt + 1)],
                            v_h[128 * kt:128 * (kt + 1),
                                128 * kvh:128 * (kvh + 1)], s_qa)
                        dma("sync", vsb_l[:, 128 * kt:128 * (kt + 1)],
                            v_l[128 * kt:128 * (kt + 1),
                                128 * kvh:128 * (kvh + 1)], s_qa)
                    kv_t = s_qa.n
                    for s in range(S_LIM or NS):
                        ext = (s + 1) * 512
                        sev_base = s_sev.n
                        if P4_STAGE < 1:
                            continue
                        for hh in range(4):
                            head = 4 * kvh + hh
                            eq.wt("sync", s_av, s_av.n)
                            dma("sync", qsb_h[hh][:],
                                q_h[128 * head:128 * (head + 1),
                                    128 * s:128 * (s + 1)], s_qa)
                            dma("sync", qsb_l[hh][:],
                                q_l[128 * head:128 * (head + 1),
                                    128 * s:128 * (s + 1)], s_qa)
                            q_t = s_qa.n
                            e2_before = s_e2.n
                            if P4_STAGE < 1.2:
                                continue
                            for kb in range(s + 1):
                                pb = nkb[0] % 2
                                nkb[0] += 1
                                eq.wt("tensor", s_qa, q_t)
                                eq.wt("tensor", s_sev, max(0, s_sev.n - 1))
                                mm3(psc[pb][:], qsb_h[hh][:], qsb_l[hh][:],
                                    ksb_h[:, 512 * kb:512 * (kb + 1)],
                                    ksb_l[:, 512 * kb:512 * (kb + 1)],
                                    start=True, stop=True, sm=s_mm)
                                if P4_STAGE < 1.7:
                                    continue
                                eq.wt("vector", s_mm, s_mm.n)
                                eq.wt("vector", s_e2, e2_before)
                                dst = srow[:, 512 * kb:512 * (kb + 1)]
                                if kb == s:
                                    vop(lambda e, d=dst, a=psc[pb][:]:
                                        e.tensor_tensor(out=d, in0=a, in1=m01s[:],
                                                        op=AOP.mult), s_vv)
                                    eq.wt("vector", s_vv, s_vv.n)
                                    vop(lambda e, d=dst: e.tensor_tensor(
                                        out=d, in0=d, in1=madds[:], op=AOP.add),
                                        s_sev)
                                else:
                                    vop(lambda e, d=dst, a=psc[pb][:]:
                                        e.tensor_copy(d, a), s_sev)
                            if P4_STAGE < 2:
                                continue
                            eq.wt("vector", s_sev, s_sev.n)
                            vop(lambda e, w=ext: e.reduce_max(
                                nm[:], srow[:, 0:w], axis=AXX, negate=True),
                                s_vv)
                            eq.wt("vector", s_vv, s_vv.n)
                            vop(lambda e: e.tensor_scalar(
                                out=nms[:], in0=nm[:], scalar1=ISQ, scalar2=None,
                                op0=AOP.mult), s_sm)
                            eq.wt("scalar", s_sm, s_sm.n)
                            eq.wt("scalar", s_esp, s_esp.n)
                            aop(lambda e, w=ext: e.activation(
                                out=ebuf[:, 0:w], in_=srow[:, 0:w], func=AF.Exp,
                                bias=nms[:], scale=isq_ap[:],
                                accum_out=ell4[:]), s_e2)
                            eq.wt("vector", s_e2, s_e2.n)
                            vop(lambda e: e.reciprocal(lnl[:], ell4[:]), s_vv)
                            eq.wt("vector", s_vv, s_vv.n)
                            if P4_STAGE < 3:
                                continue
                            eq.wt("vector", s_tt, s_tt.n)
                            vop(lambda e, w=ext, o=eh[hh][:, 0:ext]:
                                e.tensor_scalar(out=o, in0=ebuf[:, 0:w],
                                                scalar1=lnl[:], scalar2=None,
                                                op0=AOP.mult), s_vv)
                            eq.wt("vector", s_vv, s_vv.n)
                            vop(lambda e, w=ext, o=el[hh][:, 0:ext],
                                h_=eh[hh][:, 0:ext]: e.scalar_tensor_tensor(
                                    out=o, in0=ebuf[:, 0:w], scalar=lnl[:],
                                    in1=h_, op0=AOP.mult, op1=AOP.subtract),
                                s_esp)
                        if P4_STAGE < 4:
                            continue
                        eq.wt("tensor", s_esp, s_esp.n)
                        n_ks = 4 * (s + 1)
                        tte_base = s_tte.n
                        avk_base = s_avk.n
                        for ksu in range(n_ks):
                            pb = ksu % 4
                            eq.wt("tensor", s_tte,
                                  tte_base + 2 * max(0, ksu - 3))
                            if ksu == 0:
                                eq.wt("tensor", s_ao, s_ao.n)
                                eq.wt("tensor", s_qa, kv_t)
                                eq.wt("tensor", s_tte, tte_base)
                                eq.wt("vector", s_av, s_av.n)
                            for hh in range(4):
                                top(lambda e, o=pet[pb][:, 128 * hh:128 * (hh + 1)],
                                    a=eh[hh][:, 128 * ksu:128 * (ksu + 1)]:
                                    e.transpose(o, a, idbs[:]))
                                top(lambda e, o=pet[pb][:, 512 + 128 * hh:
                                                         512 + 128 * (hh + 1)],
                                    a=el[hh][:, 128 * ksu:128 * (ksu + 1)]:
                                    e.transpose(o, a, idbs[:]),
                                    s_tt if hh == 3 else None)
                            eq.wt("vector", s_tt, s_tt.n)
                            eq.wt("vector", s_avk,
                                  avk_base + max(0, ksu - 3))
                            vop(lambda e, o=etsb_h[pb][:], a=pet[pb][:, 0:512]:
                                e.tensor_copy(o, a))
                            vop(lambda e, o=etsb_l[pb][:], a=pet[pb][:, 512:1024]:
                                e.tensor_copy(o, a), s_tte, 2)
                            if P4_STAGE < 5:
                                continue
                            eq.wt("tensor", s_tte, s_tte.n)
                            mm3(pav[:], vsb_h[:, 128 * ksu:128 * (ksu + 1)],
                                vsb_l[:, 128 * ksu:128 * (ksu + 1)],
                                etsb_h[pb][:], etsb_l[pb][:],
                                start=(ksu == 0), stop=(ksu == n_ks - 1),
                                sm=s_av if ksu == n_ks - 1 else s_avk)
                        if P4_STAGE < 6:
                            continue
                        eq.wt("vector", s_av, s_av.n)
                        eq.wt("vector", s_kv, s_kv.n)
                        split(pav[:], aostg_h[:], aostg_l[:], s_ao)
                        eq.wt("gpsimd", s_ao, s_ao.n)
                        d3h = ao_h[512 * kvh:512 * (kvh + 1),
                                   128 * s:128 * (s + 1)].rearrange(
                                       "(h p) f -> p h f", h=4, p=128)
                        d3l = ao_l[512 * kvh:512 * (kvh + 1),
                                   128 * s:128 * (s + 1)].rearrange(
                                       "(h p) f -> p h f", h=4, p=128)
                        dma("gpsimd", d3h, aostg_h[:].rearrange(
                            "p (h f) -> p h f", h=4, f=128), s_kv)
                        dma("gpsimd", d3l, aostg_l[:].rearrange(
                            "p (h f) -> p h f", h=4, f=128), s_kv)
                eq.wt("gpsimd", s_kv, s_kv.n)
                barrier()

        if PHASES >= 5:
            with ExitStack() as p5:
                SB5 = lambda n, s, d: p5.enter_context(nc.sbuf_tensor(n, s, d))
                PS5 = lambda n, s, d: p5.enter_context(nc.psum_tensor(n, s, d))
                wof = [SB5(f"wof{i}", [128, 512], F32) for i in range(2)]
                woh = [SB5(f"woh{i}", [128, 512], BF16) for i in range(2)]
                wol = [SB5(f"wol{i}", [128, 512], BF16) for i in range(2)]
                aot_h = [SB5(f"aoth{i}", [128, 1024], BF16) for i in range(2)]
                aot_l = [SB5(f"aotl{i}", [128, 1024], BF16) for i in range(2)]
                gld = [SB5(f"gld{i}", [128, 1024], F32) for i in range(2)]
                sgst = SB5("sgst", [128, 1024], F32)
                tst = [SB5(f"tst{i}", [128, 512], F32) for i in range(2)]
                po5 = [PS5(f"po5{i}", [128, 1024], F32) for i in range(4)]

                nW5 = [0]
                ntr5 = [0]
                u5_sig = []
                tte5_base = s_tte.n
                for ob in range(8):
                    for ic in range(32):
                        pb = nW5[0] % 2
                        eq.wt("gpsimd", s_ws, max(0, s_ws.n - 1))
                        dma("gpsimd", wof[pb][:],
                            woT[128 * ic:128 * (ic + 1),
                                512 * ob:512 * (ob + 1)], s_w)
                        if nW5[0] >= 2:
                            eq.wt("sync", *u5_sig[nW5[0] - 2])
                        dma("sync", aot_h[pb][:],
                            ao_h[128 * ic:128 * (ic + 1), :], s_mv)
                        dma("sync", aot_l[pb][:],
                            ao_l[128 * ic:128 * (ic + 1), :], s_mv)
                        mvt = s_mv.n
                        eq.wt("vector", s_w, s_w.n)
                        if nW5[0] >= 2:
                            eq.wt("vector", *u5_sig[nW5[0] - 2])
                        split(wof[pb][:], woh[pb][:], wol[pb][:], s_ws)
                        nW5[0] += 1
                        eq.wt("tensor", s_ws, s_ws.n)
                        eq.wt("tensor", s_mv, mvt)
                        if ic == 0:
                            eq.wt("tensor", s_sg, s_sg.n)
                            eq.wt("tensor", s_tte, s_tte.n)
                        for qh2 in range(2):
                            for ot in range(4):
                                last = (qh2 == 1 and ot == 3)
                                mm3(po5[ot][:, 512 * qh2:512 * (qh2 + 1)],
                                    woh[pb][:, 128 * ot:128 * (ot + 1)],
                                    wol[pb][:, 128 * ot:128 * (ot + 1)],
                                    aot_h[pb][:, 512 * qh2:512 * (qh2 + 1)],
                                    aot_l[pb][:, 512 * qh2:512 * (qh2 + 1)],
                                    start=(ic == 0), stop=(ic == 31),
                                    sm=(s_mm if ic == 31 else s_tm)
                                    if last else None,
                                    k=1)
                        u5_sig.append((s_mm, s_mm.n) if ic == 31
                                      else (s_tm, s_tm.n))
                    for ot in range(4):
                        otile = 4 * ob + ot
                        gpb = ot % 2
                        eq.wt("sync", s_sg, max(0, s_sg.n - 1))
                        dma("sync", gld[gpb][:],
                            g_s[128 * otile:128 * (otile + 1), :], s_mv)
                        eq.wt("vector", s_mm, s_mm.n)
                        eq.wt("vector", s_mv, s_mv.n)
                        eq.wt("vector", s_tt, s_tt.n)
                        vop(lambda e, a=po5[ot][:], g=gld[gpb][:]:
                            e.tensor_tensor(out=sgst[:], in0=a, in1=g,
                                            op=AOP.mult), s_sg)
                        eq.wt("tensor", s_sg, s_sg.n)
                        for grp in range(2):
                            ptr5 = po5[ot][:, 512 * grp:512 * (grp + 1)]
                            eq.wt("tensor", s_tte,
                                  tte5_base + max(0, ntr5[0] - 1))
                            for u in range(4):
                                uu = 4 * grp + u
                                top(lambda e, o=ptr5[:, 128 * u:128 * (u + 1)],
                                    a=sgst[:, 128 * uu:128 * (uu + 1)]:
                                    e.transpose(o, a, idfs[:]),
                                    s_tt if u == 3 else None)
                            eq.wt("scalar", s_tt, s_tt.n)
                            eq.wt("scalar", s_o, 16 * max(0, ntr5[0] - 1))
                            aop(lambda e, o=tst[ntr5[0] % 2][:], a=ptr5:
                                e.activation(out=o, in_=a, func=AF.Copy), s_tte)
                            eq.wt("gpsimd", s_tte, s_tte.n)
                            d3 = sgt[512 * grp:512 * (grp + 1),
                                     128 * otile:128 * (otile + 1)].rearrange(
                                         "(s p) f -> p s f", s=4, p=128)
                            dma("gpsimd", d3, tst[ntr5[0] % 2][:].rearrange(
                                "p (s f) -> p s f", s=4, f=128), s_o)
                            ntr5[0] += 1
                eq.wt("gpsimd", s_o, s_o.n)
                barrier()

        if PHASES >= 6:
            with ExitStack() as p6:
                SB6 = lambda n, s, d: p6.enter_context(nc.sbuf_tensor(n, s, d))
                sgl = [SB6(f"sgl{i}", [128, D], F32) for i in range(2)]
                xl6 = [SB6(f"xl6{i}", [128, D], F32) for i in range(2)]
                orow = [SB6(f"orow{i}", [128, D], F32) for i in range(2)]
                o_base = s_o.n
                sg_base = s_sg.n
                for s in range(NS):
                    pb = s % 2
                    eq.wt("sync", s_o, o_base + 16 * max(0, s - 1))
                    dma("sync", sgl[pb][:], sgt[128 * s:128 * (s + 1), :], s_mv)
                    dma("sync", xl6[pb][:], xb[512 * s:512 * s + 128, :], s_mv)
                    eq.wt("vector", s_mv, s_mv.n)
                    eq.wt("vector", s_o, o_base + 16 * max(0, s - 1))
                    vop(lambda e, o=orow[pb][:], a=xl6[pb][:], g=sgl[pb][:]:
                        e.tensor_tensor(out=o, in0=a, in1=g, op=AOP.add), s_sg)
                    eq.wt("gpsimd", s_sg, s_sg.n)
                    dma("gpsimd", out[128 * s:128 * (s + 1), :], orow[pb][:], s_o)
                eq.wt("gpsimd", s_o, s_o.n)
                eq.wt("gpsimd", s_kv, s_kv.n)
                eq.wt("gpsimd", s_g, s_g.n)
                eq.wt("gpsimd", s_st, s_st.n)

        with nc.Block() as blk:
            blk.sync(lambda e: [f(e) for f in eq.q["sync"]])
            blk.tensor(lambda e: [f(e) for f in eq.q["tensor"]])
            blk.vector(lambda e: [f(e) for f in eq.q["vector"]])
            blk.scalar(lambda e: [f(e) for f in eq.q["scalar"]])
            blk.gpsimd(lambda e: [f(e) for f in eq.q["gpsimd"]])

    return nc



def _rope_tables():
    import jax
    import jax.numpy as jnp
    ROPE_THETA = 500000.0
    factor, low_f, high_f, old_ctx = 32.0, 1.0, 4.0, 8192
    inv_freq = 1.0 / ROPE_THETA ** (np.arange(0, HD, 2, dtype=np.float32) / HD)
    low_wl, high_wl = old_ctx / low_f, old_ctx / high_f
    wavelen = 2 * math.pi / inv_freq
    inv_llama = np.where(wavelen > low_wl, inv_freq / factor, inv_freq)
    smooth = (old_ctx / wavelen - low_f) / (high_f - low_f)
    smoothed = (1 - smooth) * inv_llama / factor + smooth * inv_llama
    is_med = ~(wavelen < high_wl) & ~(wavelen > low_wl)
    inv = np.where(is_med, smoothed, inv_llama).astype(np.float32)
    try:
        cpu = jax.devices("cpu")[0]
        with jax.default_device(cpu):
            pos = jnp.arange(T, dtype=jnp.float32)
            freqs = pos[:, None] * jnp.asarray(inv)[None, :]
            emb = jnp.concatenate([freqs, freqs], axis=-1)
            cos = np.asarray(jnp.cos(emb).astype(jnp.float32))
            sin = np.asarray(jnp.sin(emb).astype(jnp.float32))
    except Exception:
        pos = np.arange(T, dtype=np.float32)
        freqs = (pos[:, None] * inv[None, :]).astype(np.float32)
        emb = np.concatenate([freqs, freqs], axis=-1)
        cos = np.cos(emb).astype(np.float32)
        sin = np.sin(emb).astype(np.float32)
    return cos, sin


def kernel(L0_attn_x, L0_attn_Wq, L0_attn_Wk, L0_attn_Wv, L0_attn_Wo,
           L0_ffn_qweight, L0_ffn_scales, L0_ffn_zeros, L0_ffn_bias):
    x = np.asarray(L0_attn_x)
    Wq = np.asarray(L0_attn_Wq)
    Wk = np.asarray(L0_attn_Wk)
    Wv = np.asarray(L0_attn_Wv)
    Wo = np.asarray(L0_attn_Wo)
    qw = np.asarray(L0_ffn_qweight)
    scales = np.asarray(L0_ffn_scales)
    zeros = np.asarray(L0_ffn_zeros)
    bias = np.asarray(L0_ffn_bias)

    cos, sin = _rope_tables()
    wqT = np.ascontiguousarray(Wq.T)
    wkvT = np.ascontiguousarray(np.concatenate([Wk.T, Wv.T], axis=1))
    woT = np.ascontiguousarray(Wo.T)
    scal_c = np.ascontiguousarray(scales[:, 0].reshape(32, 128).T)
    scal_r = np.ascontiguousarray(scales[:, 0].reshape(1, D))
    zero_r = np.ascontiguousarray(zeros[:, 0].reshape(1, D))
    bias_r = np.ascontiguousarray(bias.reshape(1, D))
    idf = np.eye(128, dtype=np.float32)
    idb = np.eye(128, dtype=ml_dtypes.bfloat16)

    nc = build()

    in_maps = []
    for j in range(8):
        b, c = j // 4, j % 4
        u = np.arange(T)
        m = u // 512
        w = (u % 512) // 128
        uu = u % 128
        tau = 512 * m + 128 * ((c + w) % 4) + uu
        xb_ = np.ascontiguousarray(x[b][tau])
        cosL = np.ascontiguousarray(cos[tau].T)
        sinL = np.ascontiguousarray(sin[tau].T)
        qq = np.arange(128)[:, None]
        wcol = np.arange(512)[None, :] // 128
        uucol = np.arange(512)[None, :] % 128
        gq = 128 * c + qq
        gk = 128 * ((c + wcol) % 4) + uucol
        allow = (gk <= gq)
        m01v = allow.astype(np.float32)
        maddv = np.where(allow, 0.0, np.float32(MASKNEG)).astype(np.float32)
        in_maps.append({
            "xb": xb_, "wqT": wqT, "wkvT": wkvT, "woT": woT, "qw": qw,
            "scal_c": scal_c, "scal_r": scal_r, "zero_r": zero_r,
            "bias_r": bias_r, "cosT": cosL, "sinT": sinL,
            "idf": idf, "idb": idb, "m01": np.ascontiguousarray(m01v),
            "madd": np.ascontiguousarray(maddv),
        })

    res = run_bass_kernel_spmd(nc, in_maps, core_ids=list(range(8)))

    outp = np.empty((B, T, D), dtype=np.float32)
    for j in range(8):
        b, c = j // 4, j % 4
        oc = res.results[j]["out"]
        for s in range(NS):
            outp[b, 512 * s + 128 * c:512 * s + 128 * c + 128, :] = \
                oc[128 * s:128 * (s + 1), :]
    globals()["_last_results"] = res
    return outp
